# version 06
# baseline (speedup 1.0000x reference)
import numpy as np
import ml_dtypes
import concourse.bass as bass
import concourse.mybir as mybir
from concourse.bass_utils import run_bass_kernel_spmd
from contextlib import ExitStack

F32 = mybir.dt.float32
F32R = mybir.dt.float32r
BF16 = mybir.dt.bfloat16
F16 = mybir.dt.float16
F8 = mybir.dt.float8e4
I32 = mybir.dt.int32
I16 = mybir.dt.int16
AF = mybir.ActivationFunctionType
ALU = mybir.AluOpType

NCORE = 8
B = 16
TLOC = 256
D = 1024
M = 256
H = 4096
CH = 32
NCHUNK = TLOC // CH
TOKC = CH * B
NTILE = (TLOC * B) // 128
TPC = NTILE // NCHUNK
DEC = 0.99
DEC32 = float(np.float32(DEC) ** np.float32(32))
MAGIC = 12582912.0
EXPMASK = -8388608
CLAMP_MIN = float(2.0 ** -10)


def quant_f32(v, axis, src32, dst16, habs16, view3, bc, view3_32, bc32, l1, am_bf, am32,
              acl, apw, asc, ain, asc16, extra_mul=None, extra_add=None,
              amax_src16=None, fuse=None):
    if amax_src16 is not None:
        v.tensor_single_scalar(habs16.bitcast(I16), amax_src16.bitcast(I16), 0x7FFF, ALU.bitwise_and)
    else:
        v.scalar_tensor_tensor(habs16, src32, -1.0, src32, ALU.mult, ALU.max)
    v.drain()
    a3 = view3(habs16)
    if axis == 2:
        v.tensor_tensor(l1, a3[:, :, 0:16], a3[:, :, 16:32], ALU.max)
        v.drain()
        v.tensor_tensor(l1[:, :, 0:8], l1[:, :, 0:8], l1[:, :, 8:16], ALU.max)
        v.drain()
        v.tensor_tensor(l1[:, :, 0:4], l1[:, :, 0:4], l1[:, :, 4:8], ALU.max)
        v.drain()
        v.tensor_tensor(l1[:, :, 0:2], l1[:, :, 0:2], l1[:, :, 2:4], ALU.max)
        v.drain()
        v.tensor_tensor(am_bf.unsqueeze(2), l1[:, :, 0:1], l1[:, :, 1:2], ALU.max)
    else:
        v.tensor_tensor(l1, a3[:, 0:16, :], a3[:, 16:32, :], ALU.max)
        v.drain()
        v.tensor_tensor(l1[:, 0:8, :], l1[:, 0:8, :], l1[:, 8:16, :], ALU.max)
        v.drain()
        v.tensor_tensor(l1[:, 0:4, :], l1[:, 0:4, :], l1[:, 4:8, :], ALU.max)
        v.drain()
        v.tensor_tensor(l1[:, 0:2, :], l1[:, 0:2, :], l1[:, 2:4, :], ALU.max)
        v.drain()
        v.tensor_tensor(am_bf.unsqueeze(1), l1[:, 0:1, :], l1[:, 1:2, :], ALU.max)
    v.drain()
    v.tensor_copy(am32, am_bf)
    v.drain()
    v.tensor_single_scalar(acl, am32, CLAMP_MIN, ALU.max)
    v.drain()
    v.tensor_single_scalar(apw.bitcast(I32), acl.bitcast(I32), EXPMASK, ALU.bitwise_and)
    v.drain()
    v.tensor_single_scalar(asc.bitcast(I32), apw.bitcast(I32), 3 << 23, ALU.subtract)
    v.drain()
    v.tensor_scalar(ain.bitcast(I32), asc.bitcast(I32), -1, 254 << 23, ALU.mult, ALU.add)
    v.drain()
    v.tensor_copy(asc16, asc)
    v.drain()
    if fuse is not None:
        srstd_col, negmr_col, rinv_b, nminv_b = fuse
        v.tensor_scalar(rinv_b, ain, srstd_col, None, ALU.mult)
        v.tensor_scalar(nminv_b, ain, negmr_col, None, ALU.mult)
        v.drain()
        extra_mul, extra_add = rinv_b, nminv_b
    if extra_mul is not None:
        v.tensor_tensor(view3_32(src32), view3_32(src32), bc32(extra_mul), ALU.mult)
        v.drain()
        v.tensor_tensor(view3_32(src32), view3_32(src32), bc32(extra_add), ALU.add)
        v.drain()
    else:
        v.tensor_tensor(view3_32(src32), view3_32(src32), bc32(ain), ALU.mult)
        v.drain()
    v.tensor_scalar(habs16, src32, MAGIC, MAGIC, ALU.add, ALU.subtract)
    v.drain()
    v.tensor_tensor(view3(dst16), view3(habs16), bc(asc16), ALU.mult)
    v.drain()


def quant_chain(v, axis, src_bf, dst_bf, view3, bc, l1, am_bf, am32, acl, apw, asc, ain,
                asc16=None, ain16=None, habs=None, pre_round_wait=None):
    if habs is None:
        habs = dst_bf
    v.tensor_single_scalar(habs.bitcast(I16), src_bf.bitcast(I16), 0x7FFF, ALU.bitwise_and)
    v.drain()
    a3 = view3(habs)
    if axis == 2:
        v.tensor_tensor(l1, a3[:, :, 0:16], a3[:, :, 16:32], ALU.max)
        v.drain()
        v.tensor_tensor(l1[:, :, 0:8], l1[:, :, 0:8], l1[:, :, 8:16], ALU.max)
        v.drain()
        v.tensor_tensor(l1[:, :, 0:4], l1[:, :, 0:4], l1[:, :, 4:8], ALU.max)
        v.drain()
        v.tensor_tensor(l1[:, :, 0:2], l1[:, :, 0:2], l1[:, :, 2:4], ALU.max)
        v.drain()
        v.tensor_tensor(am_bf.unsqueeze(2), l1[:, :, 0:1], l1[:, :, 1:2], ALU.max)
    else:
        v.tensor_tensor(l1, a3[:, 0:16, :], a3[:, 16:32, :], ALU.max)
        v.drain()
        v.tensor_tensor(l1[:, 0:8, :], l1[:, 0:8, :], l1[:, 8:16, :], ALU.max)
        v.drain()
        v.tensor_tensor(l1[:, 0:4, :], l1[:, 0:4, :], l1[:, 4:8, :], ALU.max)
        v.drain()
        v.tensor_tensor(l1[:, 0:2, :], l1[:, 0:2, :], l1[:, 2:4, :], ALU.max)
        v.drain()
        v.tensor_tensor(am_bf.unsqueeze(1), l1[:, 0:1, :], l1[:, 1:2, :], ALU.max)
    v.drain()
    v.tensor_copy(am32, am_bf)
    v.drain()
    v.tensor_single_scalar(acl, am32, CLAMP_MIN, ALU.max)
    v.drain()
    v.tensor_single_scalar(apw.bitcast(I32), acl.bitcast(I32), EXPMASK, ALU.bitwise_and)
    v.drain()
    v.tensor_single_scalar(asc.bitcast(I32), apw.bitcast(I32), 3 << 23, ALU.subtract)
    v.drain()
    v.tensor_scalar(ain.bitcast(I32), asc.bitcast(I32), -1, 254 << 23, ALU.mult, ALU.add)
    v.drain()
    v.tensor_copy(asc16, asc)
    v.tensor_copy(ain16, ain)
    v.drain()
    v.tensor_tensor(view3(habs), view3(src_bf), bc(ain16), ALU.mult)
    v.drain()
    if pre_round_wait is not None:
        pre_round_wait()
    v.tensor_scalar(src_bf, habs, MAGIC, MAGIC, ALU.add, ALU.subtract)
    v.drain()
    v.tensor_tensor(view3(dst_bf), view3(src_bf), bc(asc16), ALU.mult)
    v.drain()


def build_nc():
    nc = bass.Bass("TRN2", target_bir_lowering=False)

    x_h = nc.dram_tensor("x", [B, TLOC, D], F32, kind="ExternalInput")
    wqt_h = nc.dram_tensor("wqt", [D, M], F32, kind="ExternalInput")
    wkt_h = nc.dram_tensor("wkt", [D, M], F32, kind="ExternalInput")
    wvt_h = nc.dram_tensor("wvt", [D, M], F32, kind="ExternalInput")
    wot_h = nc.dram_tensor("wot", [M, D], F32, kind="ExternalInput")
    m0t_h = nc.dram_tensor("m0t", [M, M], F32, kind="ExternalInput")
    gate_h = nc.dram_tensor("gate", [1, 1], F32, kind="ExternalInput")
    wfc_h = nc.dram_tensor("wfc", [H, D], F32, kind="ExternalInput")
    bfc_h = nc.dram_tensor("bfc", [1, H], F32, kind="ExternalInput")
    wproj_h = nc.dram_tensor("wproj", [D, H], F32, kind="ExternalInput")
    bproj_h = nc.dram_tensor("bproj", [1, D], F32, kind="ExternalInput")
    dq_h = nc.dram_tensor("dq", [128, TOKC], F32, kind="ExternalInput")
    dk_h = nc.dram_tensor("dk", [128, TOKC], F32, kind="ExternalInput")
    mask_h = nc.dram_tensor("maskt", [128, TPC, TOKC], F16, kind="ExternalInput")
    gvt_h = nc.dram_tensor("gvt", [128, 9], F32, kind="ExternalInput")
    ones_h = nc.dram_tensor("onesr", [1, 128], F32, kind="ExternalInput")
    idn_h = nc.dram_tensor("idn", [128, 128], F16, kind="ExternalInput")
    out_h = nc.dram_tensor("out", [B, TLOC, D], F32, kind="ExternalOutput")

    zin_h = nc.dram_tensor("zin", [M, M], F16)
    zout_h = nc.dram_tensor("zout", [NCORE * M, M], F16, addr_space="Shared")

    ctx = ExitStack()
    sb = lambda n, s, d: ctx.enter_context(nc.sbuf_tensor(n, s, d))
    sem = lambda n: ctx.enter_context(nc.semaphore(n))

    with ctx:
        xqT = sb("xqT", [128, 8, NTILE * 128], F8)
        wfcT = sb("wfcT", [128, 8, H], F8)
        rT = sb("rT", [128, 2, NTILE * 128], F16)
        wot_bf = sb("wot_bf", [128, 2, D], F16)
        barena = sb("barena", [1, 5376], F16)
        sgate_col = sb("sgate_col", [128, 1], F32)
        idn_sb = sb("idn_sb", [128, 128], F16)
        eps_col = sb("eps_col", [128, 1], F32)
        bfc_r = barena[0:1, 0:H]
        bpr_r = barena[0:1, H:H + D]
        ones_r = barena[0:1, H + D:H + D + 128]
        sig_b = barena[0:1, H + D + 128:H + D + 129]

        swl = sem("swl"); sSIG = sem("sSIG"); sSGR = sem("sSGR"); sGMM = sem("sGMM")
        sWCV = sem("sWCV"); swfl = sem("swfl"); sWQD = sem("sWQD"); sWT = sem("sWT")
        sW8 = sem("sW8"); sxl = sem("sxl"); sBNA = sem("sBNA"); sRSD = sem("sRSD")
        sNM = sem("sNM"); sAPL = sem("sAPL"); sXNT = sem("sXNT"); sXQ = sem("sXQ")
        sXQT = sem("sXQT"); sXQ8 = sem("sXQ8"); sQMM = sem("sQMM"); sKMM = sem("sKMM")
        sVMM = sem("sVMM"); sQE = sem("sQE"); sKE = sem("sKE"); sVE = sem("sVE")
        sKTC = sem("sKTC"); sAMM = sem("sAMM"); sAE = sem("sAE"); sRMM = sem("sRMM")
        sRE = sem("sRE"); sUMM = sem("sUMM"); sUE = sem("sUE"); sZC = sem("sZC")
        sCCD = sem("sCCD"); sCC = sem("sCC"); sSTI = sem("sSTI"); sSTB = sem("sSTB")
        sRIM = sem("sRIM"); sRIE = sem("sRIE"); sWPL = sem("sWPL"); sPQ = sem("sPQ")
        sWPT = sem("sWPT"); sxl2 = sem("sxl2"); sH1 = sem("sH1"); sGEL = sem("sGEL")
        sHQ = sem("sHQ"); sHQT = sem("sHQT"); sPOM = sem("sPOM"); sSIL = sem("sSIL")
        sSMM = sem("sSMM"); sCMB = sem("sCMB"); sOUT = sem("sOUT")
        sXNM = sem("sXNM"); sXNE = sem("sXNE"); swlg = sem("swlg")
        swfl2 = [sem("swflA"), sem("swflB")]
        sWPL2 = [sem("sWPLA"), sem("sWPLB")]
        sCCz = [sem("sCCzA"), sem("sCCzB")]
        sxlp = [sem("sxlA"), sem("sxlB")]

        with ExitStack() as w1:
            sbw = lambda n, s, d: w1.enter_context(nc.sbuf_tensor(n, s, d))
            sgin = sbw("sgin", [1, 1], F32)
            psg = w1.enter_context(nc.psum_tensor("psg", [128, 1], F32))
            blk = w1.enter_context(nc.Block("w1a"))

            @blk.gpsimd
            def _(g):
                g.dma_start(out=bfc_r, in_=bfc_h.ap()).then_inc(swlg, 16)
                g.dma_start(out=bpr_r, in_=bproj_h.ap()).then_inc(swlg, 16)
                g.dma_start(out=ones_r, in_=ones_h.ap()).then_inc(swlg, 16)

            @blk.sync
            def _(s):
                s.dma_start(out=sgin[:, :], in_=gate_h.ap()).then_inc(swl, 16)
                s.dma_start(out=idn_sb[:, :], in_=idn_h.ap()).then_inc(swl, 16)

            @blk.scalar
            def _(s):
                s.wait_ge(swl, 32)
                s.activation(sig_b, sgin[:, :], AF.Sigmoid).then_inc(sSIG, 1)

            @blk.vector
            def _(v):
                v.memset(eps_col[:, :], 1e-5)
                v.wait_ge(sGMM, 1)
                v.tensor_copy(sgate_col[:, :], psg[:, :]).then_inc(sWCV, 1)

            @blk.tensor
            def _(t):
                t.wait_ge(sSIG, 1)
                t.wait_ge(swlg, 48)
                t.matmul(psg[:, :], ones_r, sig_b, start=True, stop=True).then_inc(sGMM, 1)

        with ExitStack() as w2:
            sbw = lambda n, s, d: w2.enter_context(nc.sbuf_tensor(n, s, d))
            swf = [sbw(f"swf{i}", [128, D], F32) for i in range(2)]
            wq_bf = sbw("w_wq", [128, D], F16)
            habs = sbw("w_habs", [128, D], F16)
            l1 = sbw("w_l1", [128, 32, 16], F16)
            am_bf = sbw("w_am", [128, 32], F16)
            am32 = sbw("w_am32", [128, 32], F32)
            acl = sbw("w_acl", [128, 32], F32)
            apw = sbw("w_apw", [128, 32], F32)
            asc = sbw("w_asc", [128, 32], F32)
            ain = sbw("w_ain", [128, 32], F32)
            asc16 = sbw("w_asc16", [128, 32], F16)
            ain16 = sbw("w_ain16", [128, 32], F16)
            tst = [sbw(f"w_tst{i}", [128, 8, 128], F16) for i in range(2)]
            blk = w2.enter_context(nc.Block("w1b"))

            @blk.gpsimd
            def _(g):
                for T in range(32):
                    if T >= 2:
                        g.wait_ge(sWQD, T - 1)
                    g.dma_start(out=swf[T % 2][:, :], in_=wfc_h.ap()[128 * T:128 * (T + 1), :]).then_inc(swfl2[T % 2], 16)

            @blk.vector
            def _(v):
                view3 = lambda a: a.rearrange("p (b e) -> p b e", b=32)
                bc = lambda a: a.unsqueeze(2).broadcast_to([128, 32, 32])
                for T in range(32):
                    v.wait_ge(swfl2[T % 2], 16 * (T // 2 + 1))
                    if T >= 1:
                        v.wait_ge(sWT, 16 * T)
                    quant_f32(v, 2, swf[T % 2][:, :], wq_bf[:, :], habs[:, :], view3, bc, view3, bc,
                              l1[:, :, :], am_bf[:, :], am32[:, :], acl[:, :],
                              apw[:, :], asc[:, :], ain[:, :], asc16[:, :])
                    v.engine_nop().then_inc(sWQD, 1)
                    if T >= 1:
                        v.wait_ge(sWT, 16 * T)
                        v.tensor_copy(wfcT[:, :, 128 * (T - 1):128 * T], tst[(T - 1) % 2][:, :, :])
                        v.engine_nop().then_inc(sW8, 1)
                v.wait_ge(sWT, 16 * 32)
                v.tensor_copy(wfcT[:, :, 128 * 31:128 * 32], tst[31 % 2][:, :, :])
                v.engine_nop().then_inc(sW8, 1)

            @blk.scalar
            def _(s):
                for T in range(32):
                    s.wait_ge(sWQD, T + 1)
                    if T >= 2:
                        s.wait_ge(sW8, T - 1)
                    s.dma_start(out=tst[T % 2][:, :, :],
                                in_=wq_bf[:, :].rearrange("p (k c) -> p k c", k=8),
                                transpose=True).then_inc(sWT, 16)

        with ExitStack() as ab:
            sba = lambda n, s, d: ab.enter_context(nc.sbuf_tensor(n, s, d))
            wq_sb = sba("wq_sb", [128, 8, M], F16)
            wk_sb = sba("wk_sb", [128, 8, M], F16)
            wv_sb = sba("wv_sb", [128, 8, M], F16)
            m0_sb = sba("m0_sb", [128, 2, M], F32)
            dq_sb = sba("dq_sb", [128, TOKC], F32)
            dk_sb = sba("dk_sb", [128, TOKC], F32)
            mask_sb = sba("mask_sb", [128, TPC, TOKC], F16)
            gvt_sb = sba("gvt_sb", [128, 9], F32)
            qT_all = sba("qT_all", [128, 2, NTILE * 128], F16)
            U_all = sba("U_all", [128, 2, NCHUNK, M], F16)
            Zgt = [sba(f"Zgt{i}", [128, 2, M], F16) for i in range(2)]
            STa = sba("STa", [128, 2, M], F32)
            STb = sba("STb", [128, 2, M], F32)
            st_bf = sba("st_bf", [128, 2, M], F16)
            Za = sba("Za", [128, 2, M], F32)
            Zb = sba("Zb", [128, 2, M], F32)
            zbf = sba("zbf", [128, 2, M], F16)
            xnT = sba("xnT", [128, 8, TOKC], F16)
            ktT = sba("ktT", [128, 2, TOKC], F16)
            ktc = sba("ktc", [128, TPC, M], F16)
            v_sb = sba("v_sb", [128, TPC, M], F16)
            A_sb = sba("A_sb", [128, TPC, TOKC], F16)
            sx = [sba(f"sx{i}", [128, D], F32) for i in range(2)]
            sxn = [sba(f"sxn{i}", [128, D], F16) for i in range(2)]
            sxq = [sba(f"sxq{i}", [128, D], F16) for i in range(2)]
            xst = [sba(f"xst{i}", [128, 8, 128], F16) for i in range(2)]
            bst = sba("bst", [128, 2, 6], F32)
            bag = sba("bag", [128, 2], F32)
            slnv = sba("slnv", [128, 1], F32)
            srstd = sba("srstd", [128, 1], F32)
            negmr = sba("negmr", [128, 1], F32)
            habs = sba("a_habs", [128, D], F16)
            l1 = sba("a_l1", [128, 32, 16], F16)
            am_bf = sba("a_am", [128, 32], F16)
            am32 = sba("a_am32", [128, 32], F32)
            acl = sba("a_acl", [128, 32], F32)
            apw = sba("a_apw", [128, 32], F32)
            asc = sba("a_asc", [128, 32], F32)
            ain = sba("a_ain", [128, 32], F32)
            asc16 = sba("a_asc16", [128, 32], F16)
            ain16 = sba("a_ain16", [128, 32], F16)
            rinv = sba("a_rinv", [128, 32], F32)
            nminv = sba("a_nminv", [128, 32], F32)
            P1 = ab.enter_context(nc.psum_tensor("P1", [128, 2048], F32))
            P2 = ab.enter_context(nc.psum_tensor("P2", [128, 1024], F32))
            P3 = ab.enter_context(nc.psum_tensor("P3", [128, 512], F32))
            PT = ab.enter_context(nc.psum_tensor("PT", [128, 1024], F16))
            blk = ab.enter_context(nc.Block("phAB"))

            x_t = x_h.ap().transpose([1, 0, 2])

            @blk.sync
            def _(s):
                s.dma_start(out=m0_sb[:, :, :], in_=m0t_h.ap().rearrange("(t p) m -> p t m", p=128)).then_inc(swl, 16)
                s.dma_start(out=dq_sb[:, :], in_=dq_h.ap()).then_inc(swl, 16)
                s.dma_start(out=dk_sb[:, :], in_=dk_h.ap()).then_inc(swl, 16)
                s.dma_start(out=mask_sb[:, :, :], in_=mask_h.ap()).then_inc(swl, 16)
                s.dma_start(out=gvt_sb[:, :], in_=gvt_h.ap()).then_inc(swl, 16)
                for tt in range(NTILE):
                    if tt >= 2:
                        s.wait_ge(sXQ, tt - 1)
                    s.dma_start(out=sx[tt % 2][:, :], in_=x_t[8 * tt:8 * (tt + 1)]).then_inc(sxlp[tt % 2], 16)

            @blk.gpsimd
            def _(g):
                g.dma_start(out=wq_sb[:, :, :], in_=wqt_h.ap().rearrange("(k p) m -> p k m", p=128)).then_inc(swlg, 16)
                g.dma_start(out=wk_sb[:, :, :], in_=wkt_h.ap().rearrange("(k p) m -> p k m", p=128)).then_inc(swlg, 16)
                g.dma_start(out=wv_sb[:, :, :], in_=wvt_h.ap().rearrange("(k p) m -> p k m", p=128)).then_inc(swlg, 16)
                g.dma_start(out=wot_bf[:, :, :], in_=wot_h.ap().rearrange("(t p) h -> p t h", p=128)).then_inc(swlg, 16)
                g.wait_ge(sZC, 1)
                g.dma_start(out=zin_h.ap().rearrange("(t p) m -> p t m", p=128), in_=zbf[:, :, :]).then_inc(sCCD, 16)
                g.wait_ge(sCCD, 16)
                g.collective_compute(
                    "AllGather", ALU.bypass,
                    replica_groups=[list(range(NCORE))],
                    ins=[zin_h.ap()],
                    outs=[zout_h.ap()],
                ).then_inc(sCC)
                g.wait_ge(sCC, 1)
                for j in range(NCORE):
                    if j >= 2:
                        g.wait_ge(sSTI, j - 1)
                    g.dma_start(out=Zgt[j % 2][:, :, :],
                                in_=zout_h.ap()[M * j:M * (j + 1), :].rearrange("(t p) m -> p t m", p=128)).then_inc(sCCz[j % 2], 16)

            @blk.vector
            def _(v):
                view3 = lambda a: a.rearrange("p (b e) -> p b e", b=32)
                bc = lambda a: a.unsqueeze(2).broadcast_to([128, 32, 32])
                for c in range(NCHUNK):
                    for j in range(TPC):
                        tt = TPC * c + j
                        v.wait_ge(sxlp[tt % 2], 16 * (tt // 2 + 1))
                        if tt >= 1:
                            v.wait_ge(sRSD, tt)
                        v.bn_stats(bst[:, 0, :], sx[tt % 2][:, 0:512])
                        v.bn_stats(bst[:, 1, :], sx[tt % 2][:, 512:1024])
                        v.drain()
                        v.bn_aggr(bag[:, :], bst[:, :, :])
                        v.drain()
                        v.engine_nop().then_inc(sBNA, 1)
                        v.wait_ge(sRSD, tt + 1)
                        v.scalar_tensor_tensor(negmr[:, :], bag[:, 0:1], -1.0, srstd[:, :], ALU.mult, ALU.mult)
                        v.drain()
                        v.engine_nop().then_inc(sNM, 1)
                        v.wait_ge(sAPL, tt + 1)
                        quant_f32(v, 2, sx[tt % 2][:, :], sxq[tt % 2][:, :], habs[:, :],
                                  view3, bc, view3, bc,
                                  l1[:, :, :], am_bf[:, :], am32[:, :], acl[:, :],
                                  apw[:, :], asc[:, :], ain[:, :], asc16[:, :],
                                  amax_src16=sxn[tt % 2][:, :],
                                  fuse=(srstd[:, :], negmr[:, :], rinv[:, :], nminv[:, :]))
                        v.engine_nop().then_inc(sXQ, 1)
                        v.wait_ge(sXQT, 16 * (tt + 1))
                        v.tensor_copy(xqT[:, :, 128 * tt:128 * (tt + 1)], xst[tt % 2][:, :, :])
                        v.engine_nop().then_inc(sXQ8, 1)
                    v.wait_ge(sQMM, c + 1)
                    for mt in range(2):
                        v.tensor_tensor(qT_all[:, mt, TOKC * c:TOKC * (c + 1)],
                                        P1[:, 512 * mt:512 * (mt + 1)], dq_sb[:, :], ALU.mult)
                    v.engine_nop().then_inc(sQE, 1)
                    v.wait_ge(sKMM, c + 1)
                    for mt in range(2):
                        v.tensor_tensor(ktT[:, mt, :],
                                        P1[:, 1024 + 512 * mt:1024 + 512 * (mt + 1)], dk_sb[:, :], ALU.mult)
                    v.engine_nop().then_inc(sKE, 1)
                    v.wait_ge(sAMM, c + 1)
                    for ut in range(TPC):
                        v.tensor_tensor(A_sb[:, ut, :], P1[:, 512 * ut:512 * (ut + 1)],
                                        mask_sb[:, ut, :], ALU.mult)
                    v.engine_nop().then_inc(sAE, 1)

                v.memset(Za[:, :, :], 0.0)
                v.drain()
                for c in range(NCHUNK):
                    v.wait_ge(sUE, c + 1)
                    v.tensor_tensor(Zb[:, :, :], Za[:, :, :], U_all[:, :, c, :], ALU.add)
                    v.drain()
                    v.tensor_single_scalar(Za[:, :, :], Zb[:, :, :], DEC32, ALU.mult)
                    v.drain()
                v.tensor_copy(zbf[:, :, :], Za[:, :, :])
                v.drain()
                v.engine_nop().then_inc(sZC, 1)
                v.wait_ge(swl, 112)
                v.tensor_scalar(STa[:, :, :], m0_sb[:, :, :], gvt_sb[:, 8:9], None, ALU.mult)
                v.drain()
                for j in range(NCORE):
                    v.wait_ge(sCCz[j % 2], 16 * (j // 2 + 1))
                    v.tensor_scalar(Zb[:, :, :], Zgt[j % 2][:, :, :], gvt_sb[:, j:j + 1], None, ALU.mult)
                    v.drain()
                    v.engine_nop().then_inc(sSTI, 1)
                    v.tensor_tensor(STa[:, :, :], STa[:, :, :], Zb[:, :, :], ALU.add)
                    v.drain()
                for c in range(NCHUNK):
                    v.tensor_copy(st_bf[:, :, :], STa[:, :, :])
                    v.drain()
                    v.engine_nop().then_inc(sSTB, 1)
                    v.tensor_tensor(STb[:, :, :], STa[:, :, :], U_all[:, :, c, :], ALU.add)
                    v.drain()
                    v.tensor_single_scalar(STa[:, :, :], STb[:, :, :], DEC32, ALU.mult)
                    v.drain()
                    v.wait_ge(sRIM, c + 1)
                    for mt in range(2):
                        v.tensor_tensor(rT[:, mt, TOKC * c:TOKC * (c + 1)],
                                        rT[:, mt, TOKC * c:TOKC * (c + 1)],
                                        P2[:, 512 * mt:512 * (mt + 1)], ALU.add)
                    v.engine_nop().then_inc(sRIE, 1)

            @blk.scalar
            def _(s):
                for c in range(NCHUNK):
                    for j in range(TPC):
                        tt = TPC * c + j
                        s.wait_ge(sBNA, tt + 1)
                        s.activation(slnv[:, :], bag[:, 1:2], AF.Ln, bias=eps_col[:, :])
                        s.drain()
                        s.activation(srstd[:, :], slnv[:, :], AF.Exp, scale=-0.5)
                        s.drain()
                        s.nop().then_inc(sRSD, 1)
                        s.wait_ge(sNM, tt + 1)
                        if tt >= 2:
                            s.wait_ge(sXNM, tt - 1)
                            s.wait_ge(sXQ, tt - 1)
                        s.activation(sxn[tt % 2][:, :], sx[tt % 2][:, :], AF.Identity,
                                     bias=negmr[:, :], scale=srstd[:, :]).then_inc(sAPL, 1)
                        s.wait_ge(sXNM, tt + 1)
                        if j == 0 and c >= 1:
                            s.wait_ge(sVMM, c)
                        s.activation(xnT[:, :, 128 * j:128 * (j + 1)],
                                     PT[:, :].rearrange("p (k c2) -> p k c2", k=8),
                                     AF.Identity).then_inc(sXNE, 1)
                        s.wait_ge(sXQ, tt + 1)
                        if tt >= 2:
                            s.wait_ge(sXQ8, tt - 1)
                        s.dma_start(out=xst[tt % 2][:, :, :],
                                    in_=sxq[tt % 2][:, :].rearrange("p (k c2) -> p k c2", k=8),
                                    transpose=True).then_inc(sXQT, 16)
                    s.wait_ge(sVMM, c + 1)
                    s.activation(v_sb[:, :, :], P2[:, :].rearrange("p (a b) -> p a b", a=TPC), AF.Identity).then_inc(sVE, 1)
                    s.wait_ge(sKE, c + 1)
                    if c >= 1:
                        s.wait_ge(sUMM, c)
                    for nt in range(2):
                        s.dma_start(out=ktc[:, :, 128 * nt:128 * (nt + 1)],
                                    in_=ktT[:, nt, :].rearrange("p (u c2) -> p u c2", u=TPC),
                                    transpose=True).then_inc(sKTC, 16)
                    s.wait_ge(sRMM, c + 1)
                    s.activation(rT[:, :, TOKC * c:TOKC * (c + 1)],
                                 P2[:, :].rearrange("p (a b) -> p a b", a=2), AF.Identity).then_inc(sRE, 1)
                    s.wait_ge(sUMM, c + 1)
                    s.activation(U_all[:, :, c, :], P3[:, :].rearrange("p (a b) -> p a b", a=2), AF.Identity).then_inc(sUE, 1)

            @blk.tensor
            def _(t):
                t.wait_ge(sWCV, 1)
                t.wait_ge(swl, 112)
                t.wait_ge(swlg, 112)
                for c in range(NCHUNK):
                    for j in range(TPC):
                        tt = TPC * c + j
                        t.wait_ge(sAPL, tt + 1)
                        if tt >= 1:
                            t.wait_ge(sXNE, tt)
                        for k in range(8):
                            i = t.transpose(PT[:, 128 * k:128 * (k + 1)],
                                            sxn[tt % 2][:, 128 * k:128 * (k + 1)], idn_sb[:, :])
                        i.then_inc(sXNM, 1)
                    t.wait_ge(sXNE, TPC * (c + 1))
                    if c >= 1:
                        t.wait_ge(sAE, c)
                        t.wait_ge(sKE, c)
                    for mt in range(2):
                        for k in range(8):
                            i = t.matmul(P1[:, 512 * mt:512 * (mt + 1)], wq_sb[:, k, 128 * mt:128 * (mt + 1)],
                                         xnT[:, k, :], start=(k == 0), stop=(k == 7))
                    i.then_inc(sQMM, 1)
                    for mt in range(2):
                        for k in range(8):
                            i = t.matmul(P1[:, 1024 + 512 * mt:1024 + 512 * (mt + 1)], wk_sb[:, k, 128 * mt:128 * (mt + 1)],
                                         xnT[:, k, :], start=(k == 0), stop=(k == 7))
                    i.then_inc(sKMM, 1)
                    if c >= 1:
                        t.wait_ge(sRE, c)
                    for vt in range(TPC):
                        for k in range(8):
                            i = t.matmul(P2[:, 256 * vt:256 * (vt + 1)], xnT[:, k, 128 * vt:128 * (vt + 1)],
                                         wv_sb[:, k, :], start=(k == 0), stop=(k == 7))
                    i.then_inc(sVMM, 1)
                    t.wait_ge(sQE, c + 1)
                    t.wait_ge(sKE, c + 1)
                    for ut in range(TPC):
                        for km in range(2):
                            i = t.matmul(P1[:, 512 * ut:512 * (ut + 1)], ktT[:, km, 128 * ut:128 * (ut + 1)],
                                         qT_all[:, km, TOKC * c:TOKC * (c + 1)], start=(km == 0), stop=(km == 1))
                    i.then_inc(sAMM, 1)
                    t.wait_ge(sAE, c + 1)
                    t.wait_ge(sVE, c + 1)
                    for mt in range(2):
                        for ut in range(TPC):
                            i = t.matmul(P2[:, 512 * mt:512 * (mt + 1)], v_sb[:, ut, 128 * mt:128 * (mt + 1)],
                                         A_sb[:, ut, :], start=(ut == 0), stop=(ut == 3))
                    i.then_inc(sRMM, 1)
                    t.wait_ge(sKTC, 32 * (c + 1))
                    if c >= 1:
                        t.wait_ge(sUE, c)
                    for nt in range(2):
                        for ut in range(TPC):
                            i = t.matmul(P3[:, 256 * nt:256 * (nt + 1)], ktc[:, ut, 128 * nt:128 * (nt + 1)],
                                         v_sb[:, ut, :], start=(ut == 0), stop=(ut == 3))
                    i.then_inc(sUMM, 1)
                for c in range(NCHUNK):
                    t.wait_ge(sSTB, c + 1)
                    if c == 0:
                        t.wait_ge(sRE, NCHUNK)
                    else:
                        t.wait_ge(sRIE, c)
                    for mt in range(2):
                        for nt in range(2):
                            i = t.matmul(P2[:, 512 * mt:512 * (mt + 1)], st_bf[:, nt, 128 * mt:128 * (mt + 1)],
                                         qT_all[:, nt, TOKC * c:TOKC * (c + 1)], start=(nt == 0), stop=(nt == 1))
                    i.then_inc(sRIM, 1)

        wprojT = sb("wprojT", [128, 32, D], F16)
        with ExitStack() as c0:
            sbc = lambda n, s, d: c0.enter_context(nc.sbuf_tensor(n, s, d))
            swp = [sbc("swp0", [128, H], F32)]
            wq_bf = sbc("c_wq", [128, H], F16)
            habs = sbc("c_habs", [128, H], F16)
            l1 = sbc("c_l1", [128, 16, 128], F16)
            am_bf = sbc("c_am", [128, 128], F16)
            am32 = sbc("c_am32", [128, 128], F32)
            acl = sbc("c_acl", [128, 128], F32)
            apw = sbc("c_apw", [128, 128], F32)
            asc = sbc("c_asc", [128, 128], F32)
            ain = sbc("c_ain", [128, 128], F32)
            asc16 = sbc("c_asc16", [128, 128], F16)
            ain16 = sbc("c_ain16", [128, 128], F16)
            blk = c0.enter_context(nc.Block("c0"))

            @blk.gpsimd
            def _(g):
                for T in range(8):
                    if T >= 1:
                        g.wait_ge(sPQ, T)
                    g.dma_start(out=swp[0][:, :], in_=wproj_h.ap()[128 * T:128 * (T + 1), :]).then_inc(sWPL2[T % 2], 16)

            @blk.vector
            def _(v):
                view3 = lambda a: a.rearrange("p (b e) -> p b e", b=32)
                bc = lambda a: a.unsqueeze(1).broadcast_to([128, 32, 128])
                bc32 = bc
                for T in range(8):
                    v.wait_ge(sWPL2[T % 2], 16 * (T // 2 + 1))
                    if T >= 1:
                        v.wait_ge(sWPT, 16 * T)
                    quant_f32(v, 1, swp[0][:, :], wq_bf[:, :], habs[:, :], view3, bc, view3, bc32,
                              l1[:, :, :], am_bf[:, :], am32[:, :], acl[:, :],
                              apw[:, :], asc[:, :], ain[:, :], asc16[:, :])
                    v.engine_nop().then_inc(sPQ, 1)

            @blk.scalar
            def _(s):
                for T in range(8):
                    s.wait_ge(sPQ, T + 1)
                    s.dma_start(out=wprojT[:, :, 128 * T:128 * (T + 1)],
                                in_=wq_bf[:, :].rearrange("p (k c2) -> p k c2", k=32),
                                transpose=True).then_inc(sWPT, 16)

        with ExitStack() as c1:
            sbc = lambda n, s, d: c1.enter_context(nc.sbuf_tensor(n, s, d))
            h_bf = sbc("h_bf", [128, H], F16)
            hb2 = sbc("hb2", [128, H], F16)
            l1 = sbc("f_l1", [128, 16, 128], F16)
            am_bf = sbc("f_am", [128, 128], F16)
            am32 = sbc("f_am32", [128, 128], F32)
            acl = sbc("f_acl", [128, 128], F32)
            apw = sbc("f_apw", [128, 128], F32)
            asc = sbc("f_asc", [128, 128], F32)
            ain = sbc("f_ain", [128, 128], F32)
            asc16 = sbc("f_asc16", [128, 128], F16)
            ain16 = sbc("f_ain16", [128, 128], F16)
            hstage = sbc("hstage", [128, 32, 128], F16)
            g_bf = sbc("g_bf", [128, D], F16)
            sx2 = sbc("sx2", [128, D], F32)
            sout = sbc("sout", [128, D], F32)
            PH = c1.enter_context(nc.psum_tensor("PH", [128, 2048], F32))
            PH2 = c1.enter_context(nc.psum_tensor("PH2", [128, 2048], F32))
            blk = c1.enter_context(nc.Block("c1"))
            PO = PH[:, 0:1024]
            PS2 = PH[:, 1024:2048]

            x_t = x_h.ap().transpose([1, 0, 2])
            o_t = out_h.ap().transpose([1, 0, 2])

            @blk.sync
            def _(s):
                for tt in range(NTILE):
                    if tt >= 1:
                        s.wait_ge(sCMB, tt)
                    s.dma_start(out=sx2[:, :], in_=x_t[8 * tt:8 * (tt + 1)]).then_inc(sxl2, 16)
                    s.wait_ge(sCMB, tt + 1)
                    s.dma_start(out=o_t[8 * tt:8 * (tt + 1)], in_=sout[:, :]).then_inc(sOUT, 16)
                s.wait_ge(sOUT, 16 * NTILE)

            @blk.tensor
            def _(t):
                t.wait_ge(sW8, 32)
                t.wait_ge(sXQ8, NTILE)
                t.wait_ge(sWPT, 16 * 8)
                t.wait_ge(swlg, 112)
                for tt in range(NTILE):
                    for half, PB in ((0, PH), (1, PH2)):
                        if tt >= 1:
                            t.wait_ge(sGEL, 2 * tt)
                            if half == 0:
                                t.wait_ge(sCMB, tt)
                                t.wait_ge(sSIL, tt)
                        for k in range(8):
                            for jj in range(4):
                                t.matmul(PB[:, 512 * jj:512 * (jj + 1)],
                                         xqT[:, k, 128 * tt:128 * (tt + 1)],
                                         wfcT[:, k, 2048 * half + 512 * jj:2048 * half + 512 * (jj + 1)],
                                         start=(k == 0), stop=False)
                        for jj in range(4):
                            i = t.matmul(PB[:, 512 * jj:512 * (jj + 1)], ones_r,
                                         bfc_r[:, 2048 * half + 512 * jj:2048 * half + 512 * (jj + 1)],
                                         start=False, stop=True)
                        i.then_inc(sH1, 1)
                    t.wait_ge(sHQT, 16 * (tt + 1))
                    t.wait_ge(sGEL, 2 * tt + 1)
                    for k in range(32):
                        for nt in range(2):
                            t.matmul(PO[:, 512 * nt:512 * (nt + 1)], hstage[:, k, :],
                                     wprojT[:, k, 512 * nt:512 * (nt + 1)], start=(k == 0), stop=False)
                    for nt in range(2):
                        i = t.matmul(PO[:, 512 * nt:512 * (nt + 1)], ones_r,
                                     bpr_r[:, 512 * nt:512 * (nt + 1)], start=False, stop=True)
                    i.then_inc(sPOM, 1)
                    for nt in range(2):
                        for mt in range(2):
                            i = t.matmul(PS2[:, 512 * nt:512 * (nt + 1)],
                                         rT[:, mt, 128 * tt:128 * (tt + 1)],
                                         wot_bf[:, mt, 512 * nt:512 * (nt + 1)],
                                         start=(mt == 0), stop=(mt == 1))
                    i.then_inc(sSMM, 1)

            @blk.scalar
            def _(s):
                for tt in range(NTILE):
                    s.wait_ge(sH1, 2 * tt + 1)
                    if tt >= 1:
                        s.wait_ge(sHQ, tt)
                    s.activation(h_bf[:, 0:2048], PH[:, :], AF.Gelu).then_inc(sGEL, 1)
                    s.wait_ge(sH1, 2 * tt + 2)
                    s.activation(h_bf[:, 2048:4096], PH2[:, :], AF.Gelu).then_inc(sGEL, 1)
                    s.wait_ge(sHQ, tt + 1)
                    if tt >= 1:
                        s.wait_ge(sPOM, tt)
                    s.dma_start(out=hstage[:, :, :],
                                in_=hb2[:, :].rearrange("p (k c2) -> p k c2", k=32),
                                transpose=True).then_inc(sHQT, 16)
                    s.wait_ge(sPOM, tt + 1)
                    s.activation(g_bf[:, :], PO, AF.Silu).then_inc(sSIL, 1)

            @blk.vector
            def _(v):
                view3 = lambda a: a.rearrange("p (b e) -> p b e", b=32)
                bc = lambda a: a.unsqueeze(1).broadcast_to([128, 32, 128])
                for tt in range(NTILE):
                    v.wait_ge(sGEL, 2 * tt + 2)
                    if tt >= 1:
                        v.wait_ge(sHQT, 16 * tt)
                    quant_chain(v, 1, h_bf[:, :], hb2[:, :], view3, bc,
                                l1[:, :, :], am_bf[:, :], am32[:, :], acl[:, :],
                                apw[:, :], asc[:, :], ain[:, :], asc16[:, :], ain16[:, :], habs=None)
                    v.engine_nop().then_inc(sHQ, 1)
                    v.wait_ge(sSIL, tt + 1)
                    v.wait_ge(sSMM, tt + 1)
                    v.wait_ge(sxl2, 16 * (tt + 1))
                    if tt >= 1:
                        v.wait_ge(sOUT, 16 * tt)
                    v.scalar_tensor_tensor(sout[:, :], PS2, sgate_col[:, :], g_bf[:, :], ALU.mult, ALU.mult)
                    v.drain()
                    v.tensor_tensor(sout[:, :], sout[:, :], sx2[:, :], ALU.add)
                    v.engine_nop().then_inc(sCMB, 1)

    return nc


_PERM = None
_NC = None


def kernel(L0_attn_x, L0_attn_Wq, L0_attn_Wk, L0_attn_Wv, L0_attn_Wo, L0_attn_gate,
           L0_attn_M0, L0_ffn_w_fc, L0_ffn_b_fc, L0_ffn_w_proj, L0_ffn_b_proj):
    global _PERM, _NC
    if _PERM is None:
        p = np.arange(H)
        _PERM = 32 * (p % 128) + p // 128
    perm = _PERM
    x = np.ascontiguousarray(L0_attn_x)

    wqt = np.ascontiguousarray(np.asarray(L0_attn_Wq).T)
    wkt = np.ascontiguousarray(np.asarray(L0_attn_Wk).T)
    wvt = np.ascontiguousarray(np.asarray(L0_attn_Wv).T)
    wot = np.ascontiguousarray(np.asarray(L0_attn_Wo).T)
    m0t = np.ascontiguousarray(np.asarray(L0_attn_M0).T)
    gate = np.ascontiguousarray(np.asarray(L0_attn_gate).reshape(1, 1))
    wfc = np.ascontiguousarray(np.asarray(L0_ffn_w_fc)[perm, :])
    bfc = np.ascontiguousarray(np.asarray(L0_ffn_b_fc)[perm].reshape(1, H))
    wproj = np.ascontiguousarray(np.asarray(L0_ffn_w_proj)[:, perm])
    bproj = np.ascontiguousarray(np.asarray(L0_ffn_b_proj).reshape(1, D))

    tl = (np.arange(TOKC) // B).astype(np.float64)
    dq = np.ascontiguousarray(np.broadcast_to(
        (np.float32(DEC) ** tl).astype(np.float32), (128, TOKC)))
    dk = np.ascontiguousarray(np.broadcast_to(
        ((np.float32(DEC) ** (-1.0 - tl)) / np.float32(B)).astype(np.float32), (128, TOKC)))
    maskt = np.zeros((128, TPC, TOKC), dtype=np.float16)
    for ut in range(TPC):
        uu = ut * 128 + np.arange(128)
        maskt[:, ut, :] = (tl[uu][:, None] < tl[None, :]).astype(np.float16)
    ones = np.ones((1, 128), np.float32)
    idn = np.eye(128, dtype=np.float16)

    if _NC is None:
        _NC = build_nc()

    in_maps = []
    for i in range(NCORE):
        gv = np.zeros((9,), np.float32)
        for j in range(i):
            gv[j] = np.float32(DEC) ** np.float32(256 * (i - 1 - j))
        gv[8] = np.float32(DEC) ** np.float32(256 * i)
        gvt = np.ascontiguousarray(np.broadcast_to(gv, (128, 9)))
        in_maps.append({
            "x": np.ascontiguousarray(x[:, TLOC * i:TLOC * (i + 1), :]),
            "wqt": wqt, "wkt": wkt, "wvt": wvt, "wot": wot, "m0t": m0t, "gate": gate,
            "wfc": wfc, "bfc": bfc, "wproj": wproj, "bproj": bproj,
            "dq": dq, "dk": dk, "maskt": maskt, "gvt": gvt, "onesr": ones, "idn": idn,
        })
    res = run_bass_kernel_spmd(_NC, in_maps, core_ids=list(range(NCORE)))
    return np.concatenate([np.asarray(res.results[i]["out"]) for i in range(NCORE)], axis=1)


# version 07
# speedup vs baseline: 1.1473x; 1.1473x over previous
import numpy as np
import ml_dtypes
import concourse.bass as bass
import concourse.mybir as mybir
from concourse.bass_utils import run_bass_kernel_spmd
from contextlib import ExitStack

F32 = mybir.dt.float32
F32R = mybir.dt.float32r
BF16 = mybir.dt.bfloat16
F16 = mybir.dt.float16
F8 = mybir.dt.float8e4
I32 = mybir.dt.int32
I16 = mybir.dt.int16
AF = mybir.ActivationFunctionType
ALU = mybir.AluOpType

NCORE = 8
B = 16
TLOC = 256
D = 1024
M = 256
H = 4096
CH = 32
NCHUNK = TLOC // CH
TOKC = CH * B
NTILE = (TLOC * B) // 128
TPC = NTILE // NCHUNK
DEC = 0.99
DEC32 = float(np.float32(DEC) ** np.float32(32))
MAGIC = 12582912.0
EXPMASK = -8388608
CLAMP_MIN = float(2.0 ** -10)


def quant_f32(v, axis, src32, dst16, habs16, view3, bc, view3_32, bc32, l1, am_bf, am32,
              acl, apw, asc, ain, asc16, extra_mul=None, extra_add=None,
              amax_src16=None, fuse=None):
    if amax_src16 is not None:
        v.tensor_single_scalar(habs16.bitcast(I16), amax_src16.bitcast(I16), 0x7FFF, ALU.bitwise_and)
    else:
        v.scalar_tensor_tensor(habs16, src32, -1.0, src32, ALU.mult, ALU.max)
    v.drain()
    a3 = view3(habs16)
    if axis == 2:
        v.tensor_tensor(l1, a3[:, :, 0:16], a3[:, :, 16:32], ALU.max)
        v.drain()
        v.tensor_tensor(l1[:, :, 0:8], l1[:, :, 0:8], l1[:, :, 8:16], ALU.max)
        v.drain()
        v.tensor_tensor(l1[:, :, 0:4], l1[:, :, 0:4], l1[:, :, 4:8], ALU.max)
        v.drain()
        v.tensor_tensor(l1[:, :, 0:2], l1[:, :, 0:2], l1[:, :, 2:4], ALU.max)
        v.drain()
        v.tensor_tensor(am_bf.unsqueeze(2), l1[:, :, 0:1], l1[:, :, 1:2], ALU.max)
    else:
        v.tensor_tensor(l1, a3[:, 0:16, :], a3[:, 16:32, :], ALU.max)
        v.drain()
        v.tensor_tensor(l1[:, 0:8, :], l1[:, 0:8, :], l1[:, 8:16, :], ALU.max)
        v.drain()
        v.tensor_tensor(l1[:, 0:4, :], l1[:, 0:4, :], l1[:, 4:8, :], ALU.max)
        v.drain()
        v.tensor_tensor(l1[:, 0:2, :], l1[:, 0:2, :], l1[:, 2:4, :], ALU.max)
        v.drain()
        v.tensor_tensor(am_bf.unsqueeze(1), l1[:, 0:1, :], l1[:, 1:2, :], ALU.max)
    v.drain()
    v.tensor_copy(am32, am_bf)
    v.drain()
    v.tensor_single_scalar(acl, am32, CLAMP_MIN, ALU.max)
    v.drain()
    v.tensor_single_scalar(apw.bitcast(I32), acl.bitcast(I32), EXPMASK, ALU.bitwise_and)
    v.drain()
    v.tensor_single_scalar(asc.bitcast(I32), apw.bitcast(I32), 3 << 23, ALU.subtract)
    v.drain()
    v.tensor_scalar(ain.bitcast(I32), asc.bitcast(I32), -1, 254 << 23, ALU.mult, ALU.add)
    v.drain()
    v.tensor_copy(asc16, asc)
    v.drain()
    if fuse is not None:
        srstd_col, negmr_col, rinv_b, nminv_b = fuse
        v.tensor_scalar(rinv_b, ain, srstd_col, None, ALU.mult)
        v.tensor_scalar(nminv_b, ain, negmr_col, None, ALU.mult)
        v.drain()
        extra_mul, extra_add = rinv_b, nminv_b
    if extra_mul is not None:
        v.tensor_tensor(view3_32(src32), view3_32(src32), bc32(extra_mul), ALU.mult)
        v.drain()
        v.tensor_tensor(view3_32(src32), view3_32(src32), bc32(extra_add), ALU.add)
        v.drain()
    else:
        v.tensor_tensor(view3_32(src32), view3_32(src32), bc32(ain), ALU.mult)
        v.drain()
    v.tensor_scalar(habs16, src32, MAGIC, MAGIC, ALU.add, ALU.subtract)
    v.drain()
    v.tensor_tensor(view3(dst16), view3(habs16), bc(asc16), ALU.mult)
    v.drain()


def quant_chain(v, axis, src_bf, dst_bf, view3, bc, l1, am_bf, am32, acl, apw, asc, ain,
                asc16=None, ain16=None, habs=None, pre_round_wait=None):
    if habs is None:
        habs = dst_bf
    v.tensor_single_scalar(habs.bitcast(I16), src_bf.bitcast(I16), 0x7FFF, ALU.bitwise_and)
    v.drain()
    a3 = view3(habs)
    if axis == 2:
        v.tensor_tensor(l1, a3[:, :, 0:16], a3[:, :, 16:32], ALU.max)
        v.drain()
        v.tensor_tensor(l1[:, :, 0:8], l1[:, :, 0:8], l1[:, :, 8:16], ALU.max)
        v.drain()
        v.tensor_tensor(l1[:, :, 0:4], l1[:, :, 0:4], l1[:, :, 4:8], ALU.max)
        v.drain()
        v.tensor_tensor(l1[:, :, 0:2], l1[:, :, 0:2], l1[:, :, 2:4], ALU.max)
        v.drain()
        v.tensor_tensor(am_bf.unsqueeze(2), l1[:, :, 0:1], l1[:, :, 1:2], ALU.max)
    else:
        v.tensor_tensor(l1, a3[:, 0:16, :], a3[:, 16:32, :], ALU.max)
        v.drain()
        v.tensor_tensor(l1[:, 0:8, :], l1[:, 0:8, :], l1[:, 8:16, :], ALU.max)
        v.drain()
        v.tensor_tensor(l1[:, 0:4, :], l1[:, 0:4, :], l1[:, 4:8, :], ALU.max)
        v.drain()
        v.tensor_tensor(l1[:, 0:2, :], l1[:, 0:2, :], l1[:, 2:4, :], ALU.max)
        v.drain()
        v.tensor_tensor(am_bf.unsqueeze(1), l1[:, 0:1, :], l1[:, 1:2, :], ALU.max)
    v.drain()
    v.tensor_copy(am32, am_bf)
    v.drain()
    v.tensor_single_scalar(acl, am32, CLAMP_MIN, ALU.max)
    v.drain()
    v.tensor_single_scalar(apw.bitcast(I32), acl.bitcast(I32), EXPMASK, ALU.bitwise_and)
    v.drain()
    v.tensor_single_scalar(asc.bitcast(I32), apw.bitcast(I32), 3 << 23, ALU.subtract)
    v.drain()
    v.tensor_scalar(ain.bitcast(I32), asc.bitcast(I32), -1, 254 << 23, ALU.mult, ALU.add)
    v.drain()
    v.tensor_copy(asc16, asc)
    v.tensor_copy(ain16, ain)
    v.drain()
    v.tensor_tensor(view3(habs), view3(src_bf), bc(ain16), ALU.mult)
    v.drain()
    if pre_round_wait is not None:
        pre_round_wait()
    v.tensor_scalar(src_bf, habs, MAGIC, MAGIC, ALU.add, ALU.subtract)
    v.drain()
    v.tensor_tensor(view3(dst_bf), view3(src_bf), bc(asc16), ALU.mult)
    v.drain()


def build_nc():
    nc = bass.Bass("TRN2", target_bir_lowering=False)

    x_h = nc.dram_tensor("x", [B, TLOC, D], F32, kind="ExternalInput")
    wqt_h = nc.dram_tensor("wqt", [D, M], F32, kind="ExternalInput")
    wkt_h = nc.dram_tensor("wkt", [D, M], F32, kind="ExternalInput")
    wvt_h = nc.dram_tensor("wvt", [D, M], F32, kind="ExternalInput")
    wot_h = nc.dram_tensor("wot", [M, D], F32, kind="ExternalInput")
    m0t_h = nc.dram_tensor("m0t", [M, M], F32, kind="ExternalInput")
    gate_h = nc.dram_tensor("gate", [1, 1], F32, kind="ExternalInput")
    wfc_h = nc.dram_tensor("wfc", [H, D], F32, kind="ExternalInput")
    bfc_h = nc.dram_tensor("bfc", [1, H], F32, kind="ExternalInput")
    wproj_h = nc.dram_tensor("wproj", [D, H], F32, kind="ExternalInput")
    bproj_h = nc.dram_tensor("bproj", [1, D], F32, kind="ExternalInput")
    dq_h = nc.dram_tensor("dq", [128, TOKC], F32, kind="ExternalInput")
    dk_h = nc.dram_tensor("dk", [128, TOKC], F32, kind="ExternalInput")
    mask_h = nc.dram_tensor("maskt", [128, TPC, TOKC], F16, kind="ExternalInput")
    gvt_h = nc.dram_tensor("gvt", [128, 9], F32, kind="ExternalInput")
    ones_h = nc.dram_tensor("onesr", [1, 128], F32, kind="ExternalInput")
    idn_h = nc.dram_tensor("idn", [128, 128], F16, kind="ExternalInput")
    out_h = nc.dram_tensor("out", [B, TLOC, D], F32, kind="ExternalOutput")

    zin_h = nc.dram_tensor("zin", [M, M], F16)
    zout_h = nc.dram_tensor("zout", [NCORE * M, M], F16, addr_space="Shared")

    ctx = ExitStack()
    sb = lambda n, s, d: ctx.enter_context(nc.sbuf_tensor(n, s, d))
    sem = lambda n: ctx.enter_context(nc.semaphore(n))

    with ctx:
        xqT = sb("xqT", [128, 8, NTILE * 128], F8)
        wfcT = sb("wfcT", [128, 8, H], F8)
        rT = sb("rT", [128, 2, NTILE * 128], F16)
        wot_bf = sb("wot_bf", [128, 2, D], F16)
        barena = sb("barena", [1, 5376], F16)
        sgate_col = sb("sgate_col", [128, 1], F32)
        idn_sb = sb("idn_sb", [128, 128], F16)
        eps_col = sb("eps_col", [128, 1], F32)
        bfc_r = barena[0:1, 0:H]
        bpr_r = barena[0:1, H:H + D]
        ones_r = barena[0:1, H + D:H + D + 128]
        sig_b = barena[0:1, H + D + 128:H + D + 129]

        swl = sem("swl"); sSIG = sem("sSIG"); sSGR = sem("sSGR"); sGMM = sem("sGMM")
        sWCV = sem("sWCV"); swfl = sem("swfl"); sWQD = sem("sWQD"); sWT = sem("sWT")
        sW8 = sem("sW8"); sxl = sem("sxl"); sBNA = sem("sBNA"); sRSD = sem("sRSD")
        sNM = sem("sNM"); sAPL = sem("sAPL"); sXNT = sem("sXNT"); sXQ = sem("sXQ")
        sXQT = sem("sXQT"); sXQ8 = sem("sXQ8"); sQMM = sem("sQMM"); sKMM = sem("sKMM")
        sVMM = sem("sVMM"); sQE = sem("sQE"); sKE = sem("sKE"); sVE = sem("sVE")
        sKTC = sem("sKTC"); sAMM = sem("sAMM"); sAE = sem("sAE"); sRMM = sem("sRMM")
        sRE = sem("sRE"); sUMM = sem("sUMM"); sUE = sem("sUE"); sZC = sem("sZC")
        sCCD = sem("sCCD"); sCC = sem("sCC"); sSTI = sem("sSTI"); sSTB = sem("sSTB")
        sRIM = sem("sRIM"); sRIE = sem("sRIE"); sWPL = sem("sWPL"); sPQ = sem("sPQ")
        sWPT = sem("sWPT"); sxl2 = sem("sxl2"); sH1 = sem("sH1"); sGEL = sem("sGEL")
        sHQ = sem("sHQ"); sHQT = sem("sHQT"); sPOM = sem("sPOM"); sSIL = sem("sSIL")
        sSMM = sem("sSMM"); sCMB = sem("sCMB"); sOUT = sem("sOUT")
        sXNM = sem("sXNM"); sXNE = sem("sXNE"); swlg = sem("swlg"); sTAN = sem("sTAN")
        swfl2 = [sem("swflA"), sem("swflB")]
        sWPL2 = [sem("sWPLA"), sem("sWPLB")]
        sCCz = [sem("sCCzA"), sem("sCCzB")]
        sxlp = [sem("sxlA"), sem("sxlB")]

        with ExitStack() as w1:
            sbw = lambda n, s, d: w1.enter_context(nc.sbuf_tensor(n, s, d))
            sgin = sbw("sgin", [1, 1], F32)
            psg = w1.enter_context(nc.psum_tensor("psg", [128, 1], F32))
            blk = w1.enter_context(nc.Block("w1a"))

            @blk.gpsimd
            def _(g):
                g.dma_start(out=bfc_r, in_=bfc_h.ap()).then_inc(swlg, 16)
                g.dma_start(out=bpr_r, in_=bproj_h.ap()).then_inc(swlg, 16)
                g.dma_start(out=ones_r, in_=ones_h.ap()).then_inc(swlg, 16)

            @blk.sync
            def _(s):
                s.dma_start(out=sgin[:, :], in_=gate_h.ap()).then_inc(swl, 16)
                s.dma_start(out=idn_sb[:, :], in_=idn_h.ap()).then_inc(swl, 16)

            @blk.scalar
            def _(s):
                s.wait_ge(swl, 32)
                s.activation(sig_b, sgin[:, :], AF.Sigmoid).then_inc(sSIG, 1)

            @blk.vector
            def _(v):
                v.memset(eps_col[:, :], 1e-5)
                v.wait_ge(sGMM, 1)
                v.tensor_copy(sgate_col[:, :], psg[:, :]).then_inc(sWCV, 1)

            @blk.tensor
            def _(t):
                t.wait_ge(sSIG, 1)
                t.wait_ge(swlg, 48)
                t.matmul(psg[:, :], ones_r, sig_b, start=True, stop=True).then_inc(sGMM, 1)

        with ExitStack() as w2:
            sbw = lambda n, s, d: w2.enter_context(nc.sbuf_tensor(n, s, d))
            swf = [sbw(f"swf{i}", [128, D], F32) for i in range(2)]
            wq_bf = sbw("w_wq", [128, D], F16)
            habs = sbw("w_habs", [128, D], F16)
            l1 = sbw("w_l1", [128, 32, 16], F16)
            am_bf = sbw("w_am", [128, 32], F16)
            am32 = sbw("w_am32", [128, 32], F32)
            acl = sbw("w_acl", [128, 32], F32)
            apw = sbw("w_apw", [128, 32], F32)
            asc = sbw("w_asc", [128, 32], F32)
            ain = sbw("w_ain", [128, 32], F32)
            asc16 = sbw("w_asc16", [128, 32], F16)
            ain16 = sbw("w_ain16", [128, 32], F16)
            tst = [sbw(f"w_tst{i}", [128, 8, 128], F16) for i in range(2)]
            blk = w2.enter_context(nc.Block("w1b"))

            @blk.gpsimd
            def _(g):
                for T in range(32):
                    if T >= 2:
                        g.wait_ge(sWQD, T - 1)
                    g.dma_start(out=swf[T % 2][:, :], in_=wfc_h.ap()[128 * T:128 * (T + 1), :]).then_inc(swfl2[T % 2], 16)

            @blk.vector
            def _(v):
                view3 = lambda a: a.rearrange("p (b e) -> p b e", b=32)
                bc = lambda a: a.unsqueeze(2).broadcast_to([128, 32, 32])
                for T in range(32):
                    v.wait_ge(swfl2[T % 2], 16 * (T // 2 + 1))
                    if T >= 1:
                        v.wait_ge(sWT, 16 * T)
                    quant_f32(v, 2, swf[T % 2][:, :], wq_bf[:, :], habs[:, :], view3, bc, view3, bc,
                              l1[:, :, :], am_bf[:, :], am32[:, :], acl[:, :],
                              apw[:, :], asc[:, :], ain[:, :], asc16[:, :])
                    v.engine_nop().then_inc(sWQD, 1)
                    if T >= 1:
                        v.wait_ge(sWT, 16 * T)
                        v.tensor_copy(wfcT[:, :, 128 * (T - 1):128 * T], tst[(T - 1) % 2][:, :, :])
                        v.engine_nop().then_inc(sW8, 1)
                v.wait_ge(sWT, 16 * 32)
                v.tensor_copy(wfcT[:, :, 128 * 31:128 * 32], tst[31 % 2][:, :, :])
                v.engine_nop().then_inc(sW8, 1)

            @blk.scalar
            def _(s):
                for T in range(32):
                    s.wait_ge(sWQD, T + 1)
                    if T >= 2:
                        s.wait_ge(sW8, T - 1)
                    s.dma_start(out=tst[T % 2][:, :, :],
                                in_=wq_bf[:, :].rearrange("p (k c) -> p k c", k=8),
                                transpose=True).then_inc(sWT, 16)

        with ExitStack() as ab:
            sba = lambda n, s, d: ab.enter_context(nc.sbuf_tensor(n, s, d))
            wq_sb = sba("wq_sb", [128, 8, M], F16)
            wk_sb = sba("wk_sb", [128, 8, M], F16)
            wv_sb = sba("wv_sb", [128, 8, M], F16)
            m0_sb = sba("m0_sb", [128, 2, M], F32)
            dq_sb = sba("dq_sb", [128, TOKC], F32)
            dk_sb = sba("dk_sb", [128, TOKC], F32)
            mask_sb = sba("mask_sb", [128, TPC, TOKC], F16)
            gvt_sb = sba("gvt_sb", [128, 9], F32)
            qT_all = sba("qT_all", [128, 2, NTILE * 128], F16)
            U_all = sba("U_all", [128, 2, NCHUNK, M], F16)
            Zgt = [sba(f"Zgt{i}", [128, 2, M], F16) for i in range(2)]
            STa = sba("STa", [128, 2, M], F32)
            STb = sba("STb", [128, 2, M], F32)
            st_bf = sba("st_bf", [128, 2, M], F16)
            Za = sba("Za", [128, 2, M], F32)
            Zb = sba("Zb", [128, 2, M], F32)
            zbf = sba("zbf", [128, 2, M], F16)
            xnT = sba("xnT", [128, 8, TOKC], F16)
            ktT = sba("ktT", [128, 2, TOKC], F16)
            ktc = sba("ktc", [128, TPC, M], F16)
            v_sb = sba("v_sb", [128, TPC, M], F16)
            A_sb = sba("A_sb", [128, TPC, TOKC], F16)
            sx = [sba(f"sx{i}", [128, D], F32) for i in range(2)]
            sxn = [sba(f"sxn{i}", [128, D], F16) for i in range(2)]
            sxq = [sba(f"sxq{i}", [128, D], F16) for i in range(2)]
            xst = [sba(f"xst{i}", [128, 8, 128], F16) for i in range(2)]
            bst = sba("bst", [128, 2, 6], F32)
            bag = sba("bag", [128, 2], F32)
            slnv = sba("slnv", [128, 1], F32)
            srstd = sba("srstd", [128, 1], F32)
            negmr = sba("negmr", [128, 1], F32)
            habs = sba("a_habs", [128, D], F16)
            l1 = sba("a_l1", [128, 32, 16], F16)
            am_bf = sba("a_am", [128, 32], F16)
            am32 = sba("a_am32", [128, 32], F32)
            acl = sba("a_acl", [128, 32], F32)
            apw = sba("a_apw", [128, 32], F32)
            asc = sba("a_asc", [128, 32], F32)
            ain = sba("a_ain", [128, 32], F32)
            asc16 = sba("a_asc16", [128, 32], F16)
            ain16 = sba("a_ain16", [128, 32], F16)
            rinv = sba("a_rinv", [128, 32], F32)
            nminv = sba("a_nminv", [128, 32], F32)
            P1 = ab.enter_context(nc.psum_tensor("P1", [128, 2048], F32))
            P2 = ab.enter_context(nc.psum_tensor("P2", [128, 1024], F32))
            P3 = ab.enter_context(nc.psum_tensor("P3", [128, 512], F32))
            PT = ab.enter_context(nc.psum_tensor("PT", [128, 1024], F16))
            blk = ab.enter_context(nc.Block("phAB"))

            x_t = x_h.ap().transpose([1, 0, 2])

            @blk.sync
            def _(s):
                s.dma_start(out=m0_sb[:, :, :], in_=m0t_h.ap().rearrange("(t p) m -> p t m", p=128)).then_inc(swl, 16)
                s.dma_start(out=dq_sb[:, :], in_=dq_h.ap()).then_inc(swl, 16)
                s.dma_start(out=dk_sb[:, :], in_=dk_h.ap()).then_inc(swl, 16)
                s.dma_start(out=mask_sb[:, :, :], in_=mask_h.ap()).then_inc(swl, 16)
                s.dma_start(out=gvt_sb[:, :], in_=gvt_h.ap()).then_inc(swl, 16)
                for tt in range(NTILE):
                    if tt >= 2:
                        s.wait_ge(sXQ, tt - 1)
                    s.dma_start(out=sx[tt % 2][:, :], in_=x_t[8 * tt:8 * (tt + 1)]).then_inc(sxlp[tt % 2], 16)

            @blk.gpsimd
            def _(g):
                g.dma_start(out=wq_sb[:, :, :], in_=wqt_h.ap().rearrange("(k p) m -> p k m", p=128)).then_inc(swlg, 16)
                g.dma_start(out=wk_sb[:, :, :], in_=wkt_h.ap().rearrange("(k p) m -> p k m", p=128)).then_inc(swlg, 16)
                g.dma_start(out=wv_sb[:, :, :], in_=wvt_h.ap().rearrange("(k p) m -> p k m", p=128)).then_inc(swlg, 16)
                g.dma_start(out=wot_bf[:, :, :], in_=wot_h.ap().rearrange("(t p) h -> p t h", p=128)).then_inc(swlg, 16)
                g.wait_ge(sZC, 1)
                g.dma_start(out=zin_h.ap().rearrange("(t p) m -> p t m", p=128), in_=zbf[:, :, :]).then_inc(sCCD, 16)
                g.wait_ge(sCCD, 16)
                g.collective_compute(
                    "AllGather", ALU.bypass,
                    replica_groups=[list(range(NCORE))],
                    ins=[zin_h.ap()],
                    outs=[zout_h.ap()],
                ).then_inc(sCC)
                g.wait_ge(sCC, 1)
                for j in range(NCORE):
                    if j >= 2:
                        g.wait_ge(sSTI, j - 1)
                    g.dma_start(out=Zgt[j % 2][:, :, :],
                                in_=zout_h.ap()[M * j:M * (j + 1), :].rearrange("(t p) m -> p t m", p=128)).then_inc(sCCz[j % 2], 16)

            @blk.vector
            def _(v):
                view3 = lambda a: a.rearrange("p (b e) -> p b e", b=32)
                bc = lambda a: a.unsqueeze(2).broadcast_to([128, 32, 32])
                for c in range(NCHUNK):
                    for j in range(TPC):
                        tt = TPC * c + j
                        v.wait_ge(sxlp[tt % 2], 16 * (tt // 2 + 1))
                        if tt >= 1:
                            v.wait_ge(sRSD, tt)
                        v.bn_stats(bst[:, 0, :], sx[tt % 2][:, 0:512])
                        v.bn_stats(bst[:, 1, :], sx[tt % 2][:, 512:1024])
                        v.drain()
                        v.bn_aggr(bag[:, :], bst[:, :, :])
                        v.drain()
                        v.engine_nop().then_inc(sBNA, 1)
                        v.wait_ge(sRSD, tt + 1)
                        v.scalar_tensor_tensor(negmr[:, :], bag[:, 0:1], -1.0, srstd[:, :], ALU.mult, ALU.mult)
                        v.drain()
                        v.engine_nop().then_inc(sNM, 1)
                        v.wait_ge(sAPL, tt + 1)
                        quant_f32(v, 2, sx[tt % 2][:, :], sxq[tt % 2][:, :], habs[:, :],
                                  view3, bc, view3, bc,
                                  l1[:, :, :], am_bf[:, :], am32[:, :], acl[:, :],
                                  apw[:, :], asc[:, :], ain[:, :], asc16[:, :],
                                  amax_src16=sxn[tt % 2][:, :],
                                  fuse=(srstd[:, :], negmr[:, :], rinv[:, :], nminv[:, :]))
                        v.engine_nop().then_inc(sXQ, 1)
                        v.wait_ge(sXQT, 16 * (tt + 1))
                        v.tensor_copy(xqT[:, :, 128 * tt:128 * (tt + 1)], xst[tt % 2][:, :, :])
                        v.engine_nop().then_inc(sXQ8, 1)
                    v.wait_ge(sQMM, c + 1)
                    for mt in range(2):
                        v.tensor_tensor(qT_all[:, mt, TOKC * c:TOKC * (c + 1)],
                                        P1[:, 512 * mt:512 * (mt + 1)], dq_sb[:, :], ALU.mult)
                    v.engine_nop().then_inc(sQE, 1)
                    v.wait_ge(sKMM, c + 1)
                    for mt in range(2):
                        v.tensor_tensor(ktT[:, mt, :],
                                        P1[:, 1024 + 512 * mt:1024 + 512 * (mt + 1)], dk_sb[:, :], ALU.mult)
                    v.engine_nop().then_inc(sKE, 1)
                    v.wait_ge(sAMM, c + 1)
                    for ut in range(TPC):
                        v.tensor_tensor(A_sb[:, ut, :], P1[:, 512 * ut:512 * (ut + 1)],
                                        mask_sb[:, ut, :], ALU.mult)
                    v.engine_nop().then_inc(sAE, 1)

                v.memset(Za[:, :, :], 0.0)
                v.drain()
                for c in range(NCHUNK):
                    v.wait_ge(sUE, c + 1)
                    v.tensor_tensor(Zb[:, :, :], Za[:, :, :], U_all[:, :, c, :], ALU.add)
                    v.drain()
                    v.tensor_single_scalar(Za[:, :, :], Zb[:, :, :], DEC32, ALU.mult)
                    v.drain()
                v.tensor_copy(zbf[:, :, :], Za[:, :, :])
                v.drain()
                v.engine_nop().then_inc(sZC, 1)
                v.wait_ge(swl, 112)
                v.tensor_scalar(STa[:, :, :], m0_sb[:, :, :], gvt_sb[:, 8:9], None, ALU.mult)
                v.drain()
                for j in range(NCORE):
                    v.wait_ge(sCCz[j % 2], 16 * (j // 2 + 1))
                    v.tensor_scalar(Zb[:, :, :], Zgt[j % 2][:, :, :], gvt_sb[:, j:j + 1], None, ALU.mult)
                    v.drain()
                    v.engine_nop().then_inc(sSTI, 1)
                    v.tensor_tensor(STa[:, :, :], STa[:, :, :], Zb[:, :, :], ALU.add)
                    v.drain()
                for c in range(NCHUNK):
                    v.tensor_copy(st_bf[:, :, :], STa[:, :, :])
                    v.drain()
                    v.engine_nop().then_inc(sSTB, 1)
                    v.tensor_tensor(STb[:, :, :], STa[:, :, :], U_all[:, :, c, :], ALU.add)
                    v.drain()
                    v.tensor_single_scalar(STa[:, :, :], STb[:, :, :], DEC32, ALU.mult)
                    v.drain()
                    v.wait_ge(sRIM, c + 1)
                    for mt in range(2):
                        v.tensor_tensor(rT[:, mt, TOKC * c:TOKC * (c + 1)],
                                        rT[:, mt, TOKC * c:TOKC * (c + 1)],
                                        P2[:, 512 * mt:512 * (mt + 1)], ALU.add)
                    v.engine_nop().then_inc(sRIE, 1)

            @blk.scalar
            def _(s):
                for c in range(NCHUNK):
                    for j in range(TPC):
                        tt = TPC * c + j
                        s.wait_ge(sBNA, tt + 1)
                        s.activation(slnv[:, :], bag[:, 1:2], AF.Ln, bias=eps_col[:, :])
                        s.drain()
                        s.activation(srstd[:, :], slnv[:, :], AF.Exp, scale=-0.5)
                        s.drain()
                        s.nop().then_inc(sRSD, 1)
                        s.wait_ge(sNM, tt + 1)
                        if tt >= 2:
                            s.wait_ge(sXNM, tt - 1)
                            s.wait_ge(sXQ, tt - 1)
                        s.activation(sxn[tt % 2][:, :], sx[tt % 2][:, :], AF.Identity,
                                     bias=negmr[:, :], scale=srstd[:, :]).then_inc(sAPL, 1)
                        s.wait_ge(sXNM, tt + 1)
                        if j == 0 and c >= 1:
                            s.wait_ge(sVMM, c)
                        s.activation(xnT[:, :, 128 * j:128 * (j + 1)],
                                     PT[:, :].rearrange("p (k c2) -> p k c2", k=8),
                                     AF.Identity).then_inc(sXNE, 1)
                        s.wait_ge(sXQ, tt + 1)
                        if tt >= 2:
                            s.wait_ge(sXQ8, tt - 1)
                        s.dma_start(out=xst[tt % 2][:, :, :],
                                    in_=sxq[tt % 2][:, :].rearrange("p (k c2) -> p k c2", k=8),
                                    transpose=True).then_inc(sXQT, 16)
                    s.wait_ge(sVMM, c + 1)
                    s.activation(v_sb[:, :, :], P2[:, :].rearrange("p (a b) -> p a b", a=TPC), AF.Identity).then_inc(sVE, 1)
                    s.wait_ge(sKE, c + 1)
                    if c >= 1:
                        s.wait_ge(sUMM, c)
                    for nt in range(2):
                        s.dma_start(out=ktc[:, :, 128 * nt:128 * (nt + 1)],
                                    in_=ktT[:, nt, :].rearrange("p (u c2) -> p u c2", u=TPC),
                                    transpose=True).then_inc(sKTC, 16)
                    s.wait_ge(sRMM, c + 1)
                    s.activation(rT[:, :, TOKC * c:TOKC * (c + 1)],
                                 P2[:, :].rearrange("p (a b) -> p a b", a=2), AF.Identity).then_inc(sRE, 1)
                    s.wait_ge(sUMM, c + 1)
                    s.activation(U_all[:, :, c, :], P3[:, :].rearrange("p (a b) -> p a b", a=2), AF.Identity).then_inc(sUE, 1)

            @blk.tensor
            def _(t):
                t.wait_ge(sWCV, 1)
                t.wait_ge(swl, 112)
                t.wait_ge(swlg, 112)
                for c in range(NCHUNK):
                    for j in range(TPC):
                        tt = TPC * c + j
                        t.wait_ge(sAPL, tt + 1)
                        if tt >= 1:
                            t.wait_ge(sXNE, tt)
                        for k in range(8):
                            i = t.transpose(PT[:, 128 * k:128 * (k + 1)],
                                            sxn[tt % 2][:, 128 * k:128 * (k + 1)], idn_sb[:, :])
                        i.then_inc(sXNM, 1)
                    t.wait_ge(sXNE, TPC * (c + 1))
                    if c >= 1:
                        t.wait_ge(sAE, c)
                        t.wait_ge(sKE, c)
                    for mt in range(2):
                        for k in range(8):
                            i = t.matmul(P1[:, 512 * mt:512 * (mt + 1)], wq_sb[:, k, 128 * mt:128 * (mt + 1)],
                                         xnT[:, k, :], start=(k == 0), stop=(k == 7))
                    i.then_inc(sQMM, 1)
                    for mt in range(2):
                        for k in range(8):
                            i = t.matmul(P1[:, 1024 + 512 * mt:1024 + 512 * (mt + 1)], wk_sb[:, k, 128 * mt:128 * (mt + 1)],
                                         xnT[:, k, :], start=(k == 0), stop=(k == 7))
                    i.then_inc(sKMM, 1)
                    if c >= 1:
                        t.wait_ge(sRE, c)
                    for vt in range(TPC):
                        for k in range(8):
                            i = t.matmul(P2[:, 256 * vt:256 * (vt + 1)], xnT[:, k, 128 * vt:128 * (vt + 1)],
                                         wv_sb[:, k, :], start=(k == 0), stop=(k == 7))
                    i.then_inc(sVMM, 1)
                    t.wait_ge(sQE, c + 1)
                    t.wait_ge(sKE, c + 1)
                    for ut in range(TPC):
                        for km in range(2):
                            i = t.matmul(P1[:, 512 * ut:512 * (ut + 1)], ktT[:, km, 128 * ut:128 * (ut + 1)],
                                         qT_all[:, km, TOKC * c:TOKC * (c + 1)], start=(km == 0), stop=(km == 1))
                    i.then_inc(sAMM, 1)
                    t.wait_ge(sAE, c + 1)
                    t.wait_ge(sVE, c + 1)
                    for mt in range(2):
                        for ut in range(TPC):
                            i = t.matmul(P2[:, 512 * mt:512 * (mt + 1)], v_sb[:, ut, 128 * mt:128 * (mt + 1)],
                                         A_sb[:, ut, :], start=(ut == 0), stop=(ut == 3))
                    i.then_inc(sRMM, 1)
                    t.wait_ge(sKTC, 32 * (c + 1))
                    if c >= 1:
                        t.wait_ge(sUE, c)
                    for nt in range(2):
                        for ut in range(TPC):
                            i = t.matmul(P3[:, 256 * nt:256 * (nt + 1)], ktc[:, ut, 128 * nt:128 * (nt + 1)],
                                         v_sb[:, ut, :], start=(ut == 0), stop=(ut == 3))
                    i.then_inc(sUMM, 1)
                for c in range(NCHUNK):
                    t.wait_ge(sSTB, c + 1)
                    if c == 0:
                        t.wait_ge(sRE, NCHUNK)
                    else:
                        t.wait_ge(sRIE, c)
                    for mt in range(2):
                        for nt in range(2):
                            i = t.matmul(P2[:, 512 * mt:512 * (mt + 1)], st_bf[:, nt, 128 * mt:128 * (mt + 1)],
                                         qT_all[:, nt, TOKC * c:TOKC * (c + 1)], start=(nt == 0), stop=(nt == 1))
                    i.then_inc(sRIM, 1)

        wprojT = sb("wprojT", [128, 32, D], F16)
        with ExitStack() as c0:
            sbc = lambda n, s, d: c0.enter_context(nc.sbuf_tensor(n, s, d))
            swp = [sbc("swp0", [128, H], F32)]
            wq_bf = sbc("c_wq", [128, H], F16)
            habs = sbc("c_habs", [128, H], F16)
            l1 = sbc("c_l1", [128, 16, 128], F16)
            am_bf = sbc("c_am", [128, 128], F16)
            am32 = sbc("c_am32", [128, 128], F32)
            acl = sbc("c_acl", [128, 128], F32)
            apw = sbc("c_apw", [128, 128], F32)
            asc = sbc("c_asc", [128, 128], F32)
            ain = sbc("c_ain", [128, 128], F32)
            asc16 = sbc("c_asc16", [128, 128], F16)
            ain16 = sbc("c_ain16", [128, 128], F16)
            blk = c0.enter_context(nc.Block("c0"))

            @blk.gpsimd
            def _(g):
                for T in range(8):
                    if T >= 1:
                        g.wait_ge(sPQ, T)
                    g.dma_start(out=swp[0][:, :], in_=wproj_h.ap()[128 * T:128 * (T + 1), :]).then_inc(sWPL2[T % 2], 16)

            @blk.vector
            def _(v):
                view3 = lambda a: a.rearrange("p (b e) -> p b e", b=32)
                bc = lambda a: a.unsqueeze(1).broadcast_to([128, 32, 128])
                bc32 = bc
                for T in range(8):
                    v.wait_ge(sWPL2[T % 2], 16 * (T // 2 + 1))
                    if T >= 1:
                        v.wait_ge(sWPT, 16 * T)
                    quant_f32(v, 1, swp[0][:, :], wq_bf[:, :], habs[:, :], view3, bc, view3, bc32,
                              l1[:, :, :], am_bf[:, :], am32[:, :], acl[:, :],
                              apw[:, :], asc[:, :], ain[:, :], asc16[:, :])
                    v.engine_nop().then_inc(sPQ, 1)

            @blk.scalar
            def _(s):
                for T in range(8):
                    s.wait_ge(sPQ, T + 1)
                    s.dma_start(out=wprojT[:, :, 128 * T:128 * (T + 1)],
                                in_=wq_bf[:, :].rearrange("p (k c2) -> p k c2", k=32),
                                transpose=True).then_inc(sWPT, 16)

        with ExitStack() as c1:
            sbc = lambda n, s, d: c1.enter_context(nc.sbuf_tensor(n, s, d))
            h_bf = sbc("h_bf", [128, H], F16)
            hb2 = sbc("hb2", [128, H], F16)
            l1 = sbc("f_l1", [128, 16, 128], F16)
            am_bf = sbc("f_am", [128, 128], F16)
            am32 = sbc("f_am32", [128, 128], F32)
            acl = sbc("f_acl", [128, 128], F32)
            apw = sbc("f_apw", [128, 128], F32)
            asc = sbc("f_asc", [128, 128], F32)
            ain = sbc("f_ain", [128, 128], F32)
            asc16 = sbc("f_asc16", [128, 128], F16)
            ain16 = sbc("f_ain16", [128, 128], F16)
            hstage = sbc("hstage", [128, 32, 128], F16)
            tnh = sbc("tnh", [128, D], F16)
            g_bf = sbc("g_bf", [128, D], F16)
            sx2 = sbc("sx2", [128, D], F32)
            sout = sbc("sout", [128, D], F32)
            PH = c1.enter_context(nc.psum_tensor("PH", [128, 2048], F32))
            PH2 = c1.enter_context(nc.psum_tensor("PH2", [128, 2048], F32))
            blk = c1.enter_context(nc.Block("c1"))
            PO = PH2[:, 0:1024]
            PS2 = PH2[:, 1024:2048]

            x_t = x_h.ap().transpose([1, 0, 2])
            o_t = out_h.ap().transpose([1, 0, 2])

            def mm1_half(t, tile, half):
                for k in range(8):
                    for jj in range(4):
                        t.matmul(PH[:, 512 * jj:512 * (jj + 1)],
                                 xqT[:, k, 128 * tile:128 * (tile + 1)],
                                 wfcT[:, k, 2048 * half + 512 * jj:2048 * half + 512 * (jj + 1)],
                                 start=(k == 0), stop=False)
                for jj in range(4):
                    i = t.matmul(PH[:, 512 * jj:512 * (jj + 1)], ones_r,
                                 bfc_r[:, 2048 * half + 512 * jj:2048 * half + 512 * (jj + 1)],
                                 start=False, stop=True)
                i.then_inc(sH1, 1)

            def mm2_smm(t, u):
                if u >= 1:
                    t.wait_ge(sCMB, u)
                for nt in range(2):
                    for mt in range(2):
                        i = t.matmul(PS2[:, 512 * nt:512 * (nt + 1)],
                                     rT[:, mt, 128 * u:128 * (u + 1)],
                                     wot_bf[:, mt, 512 * nt:512 * (nt + 1)],
                                     start=(mt == 0), stop=(mt == 1))
                i.then_inc(sSMM, 1)
                t.wait_ge(sHQT, 16 * (u + 1))
                if u >= 1:
                    t.wait_ge(sSIL, u)
                for k in range(32):
                    for nt in range(2):
                        t.matmul(PO[:, 512 * nt:512 * (nt + 1)], hstage[:, k, :],
                                 wprojT[:, k, 512 * nt:512 * (nt + 1)], start=(k == 0), stop=False)
                for nt in range(2):
                    i = t.matmul(PO[:, 512 * nt:512 * (nt + 1)], ones_r,
                                 bpr_r[:, 512 * nt:512 * (nt + 1)], start=False, stop=True)
                i.then_inc(sPOM, 1)

            @blk.sync
            def _(s):
                for tt in range(NTILE):
                    if tt >= 1:
                        s.wait_ge(sCMB, tt)
                    s.dma_start(out=sx2[:, :], in_=x_t[8 * tt:8 * (tt + 1)]).then_inc(sxl2, 16)
                    s.wait_ge(sHQ, tt + 1)
                    if tt >= 1:
                        s.wait_ge(sPOM, tt)
                    s.dma_start(out=hstage[:, :, :],
                                in_=hb2[:, :].rearrange("p (k c2) -> p k c2", k=32),
                                transpose=True).then_inc(sHQT, 16)
                    if tt >= 1:
                        s.wait_ge(sCMB, tt)
                        s.dma_start(out=o_t[8 * (tt - 1):8 * tt], in_=sout[:, :]).then_inc(sOUT, 16)
                s.wait_ge(sCMB, NTILE)
                s.dma_start(out=o_t[8 * (NTILE - 1):8 * NTILE], in_=sout[:, :]).then_inc(sOUT, 16)
                s.wait_ge(sOUT, 16 * NTILE)

            @blk.tensor
            def _(t):
                t.wait_ge(sW8, 32)
                t.wait_ge(sXQ8, NTILE)
                t.wait_ge(sWPT, 16 * 8)
                t.wait_ge(swlg, 112)
                for tt in range(NTILE + 1):
                    if tt < NTILE:
                        t.wait_ge(sGEL, 2 * tt)
                        mm1_half(t, tt, 0)
                    if tt >= 1:
                        mm2_smm(t, tt - 1)
                    if tt < NTILE:
                        t.wait_ge(sGEL, 2 * tt + 1)
                        mm1_half(t, tt, 1)

            @blk.scalar
            def _(s):
                for tt in range(NTILE):
                    s.wait_ge(sH1, 2 * tt + 1)
                    if tt >= 1:
                        s.wait_ge(sHQ, tt)
                    s.activation(h_bf[:, 0:2048], PH[:, :], AF.Gelu).then_inc(sGEL, 1)
                    if tt >= 1:
                        s.wait_ge(sPOM, tt)
                        s.activation(tnh[:, :], PO, AF.Tanh, scale=0.5).then_inc(sTAN, 1)
                    s.wait_ge(sH1, 2 * tt + 2)
                    s.activation(h_bf[:, 2048:4096], PH[:, :], AF.Gelu).then_inc(sGEL, 1)
                s.wait_ge(sPOM, NTILE)
                s.activation(tnh[:, :], PO, AF.Tanh, scale=0.5).then_inc(sTAN, 1)

            @blk.vector
            def _(v):
                view3 = lambda a: a.rearrange("p (b e) -> p b e", b=32)
                bc = lambda a: a.unsqueeze(1).broadcast_to([128, 32, 128])
                for tt in range(NTILE + 1):
                    if tt >= 1:
                        u = tt - 1
                        v.wait_ge(sTAN, u + 1)
                        v.tensor_scalar(tnh[:, :], tnh[:, :], 0.5, 0.5, ALU.mult, ALU.add)
                        v.drain()
                        v.tensor_tensor(g_bf[:, :], tnh[:, :], PO, ALU.mult)
                        v.drain()
                        v.engine_nop().then_inc(sSIL, 1)
                        v.wait_ge(sSMM, u + 1)
                        v.wait_ge(sxl2, 16 * (u + 1))
                        if u >= 1:
                            v.wait_ge(sOUT, 16 * u)
                        v.scalar_tensor_tensor(sout[:, :], PS2, sgate_col[:, :], g_bf[:, :], ALU.mult, ALU.mult)
                        v.drain()
                        v.tensor_tensor(sout[:, :], sout[:, :], sx2[:, :], ALU.add)
                        v.engine_nop().then_inc(sCMB, 1)
                    if tt < NTILE:
                        v.wait_ge(sGEL, 2 * tt + 2)
                        if tt >= 1:
                            v.wait_ge(sHQT, 16 * tt)
                        quant_chain(v, 1, h_bf[:, :], hb2[:, :], view3, bc,
                                    l1[:, :, :], am_bf[:, :], am32[:, :], acl[:, :],
                                    apw[:, :], asc[:, :], ain[:, :], asc16[:, :], ain16[:, :], habs=None)
                        v.engine_nop().then_inc(sHQ, 1)

    return nc


_PERM = None
_NC = None


def kernel(L0_attn_x, L0_attn_Wq, L0_attn_Wk, L0_attn_Wv, L0_attn_Wo, L0_attn_gate,
           L0_attn_M0, L0_ffn_w_fc, L0_ffn_b_fc, L0_ffn_w_proj, L0_ffn_b_proj):
    global _PERM, _NC
    if _PERM is None:
        p = np.arange(H)
        _PERM = 32 * (p % 128) + p // 128
    perm = _PERM
    x = np.ascontiguousarray(L0_attn_x)

    wqt = np.ascontiguousarray(np.asarray(L0_attn_Wq).T)
    wkt = np.ascontiguousarray(np.asarray(L0_attn_Wk).T)
    wvt = np.ascontiguousarray(np.asarray(L0_attn_Wv).T)
    wot = np.ascontiguousarray(np.asarray(L0_attn_Wo).T)
    m0t = np.ascontiguousarray(np.asarray(L0_attn_M0).T)
    gate = np.ascontiguousarray(np.asarray(L0_attn_gate).reshape(1, 1))
    wfc = np.ascontiguousarray(np.asarray(L0_ffn_w_fc)[perm, :])
    bfc = np.ascontiguousarray(np.asarray(L0_ffn_b_fc)[perm].reshape(1, H))
    wproj = np.ascontiguousarray(np.asarray(L0_ffn_w_proj)[:, perm])
    bproj = np.ascontiguousarray(np.asarray(L0_ffn_b_proj).reshape(1, D))

    tl = (np.arange(TOKC) // B).astype(np.float64)
    dq = np.ascontiguousarray(np.broadcast_to(
        (np.float32(DEC) ** tl).astype(np.float32), (128, TOKC)))
    dk = np.ascontiguousarray(np.broadcast_to(
        ((np.float32(DEC) ** (-1.0 - tl)) / np.float32(B)).astype(np.float32), (128, TOKC)))
    maskt = np.zeros((128, TPC, TOKC), dtype=np.float16)
    for ut in range(TPC):
        uu = ut * 128 + np.arange(128)
        maskt[:, ut, :] = (tl[uu][:, None] < tl[None, :]).astype(np.float16)
    ones = np.ones((1, 128), np.float32)
    idn = np.eye(128, dtype=np.float16)

    if _NC is None:
        _NC = build_nc()

    in_maps = []
    for i in range(NCORE):
        gv = np.zeros((9,), np.float32)
        for j in range(i):
            gv[j] = np.float32(DEC) ** np.float32(256 * (i - 1 - j))
        gv[8] = np.float32(DEC) ** np.float32(256 * i)
        gvt = np.ascontiguousarray(np.broadcast_to(gv, (128, 9)))
        in_maps.append({
            "x": np.ascontiguousarray(x[:, TLOC * i:TLOC * (i + 1), :]),
            "wqt": wqt, "wkt": wkt, "wvt": wvt, "wot": wot, "m0t": m0t, "gate": gate,
            "wfc": wfc, "bfc": bfc, "wproj": wproj, "bproj": bproj,
            "dq": dq, "dk": dk, "maskt": maskt, "gvt": gvt, "onesr": ones, "idn": idn,
        })
    res = run_bass_kernel_spmd(_NC, in_maps, core_ids=list(range(NCORE)))
    return np.concatenate([np.asarray(res.results[i]["out"]) for i in range(NCORE)], axis=1)


# version 08
# speedup vs baseline: 1.2280x; 1.0704x over previous
import numpy as np
import ml_dtypes
import concourse.bass as bass
import concourse.mybir as mybir
from concourse.bass_utils import run_bass_kernel_spmd
from contextlib import ExitStack

F32 = mybir.dt.float32
F32R = mybir.dt.float32r
BF16 = mybir.dt.bfloat16
F16 = mybir.dt.float16
F8 = mybir.dt.float8e4
PM = mybir.MatmulPerfMode.DoubleRow
I32 = mybir.dt.int32
I16 = mybir.dt.int16
AF = mybir.ActivationFunctionType
ALU = mybir.AluOpType

NCORE = 8
B = 16
TLOC = 256
D = 1024
M = 256
H = 4096
CH = 32
NCHUNK = TLOC // CH
TOKC = CH * B
NTILE = (TLOC * B) // 128
TPC = NTILE // NCHUNK
DEC = 0.99
DEC32 = float(np.float32(DEC) ** np.float32(32))
MAGIC = 12582912.0
EXPMASK = -8388608
CLAMP_MIN = float(2.0 ** -10)


def quant_f32(v, axis, src32, dst16, habs16, view3, bc, view3_32, bc32, l1, am_bf, am32,
              acl, apw, asc, ain, asc16, extra_mul=None, extra_add=None,
              amax_src16=None, fuse=None):
    if amax_src16 is not None:
        v.tensor_single_scalar(habs16.bitcast(I16), amax_src16.bitcast(I16), 0x7FFF, ALU.bitwise_and)
    else:
        v.scalar_tensor_tensor(habs16, src32, -1.0, src32, ALU.mult, ALU.max)
    v.drain()
    a3 = view3(habs16)
    if axis == 2:
        v.tensor_tensor(l1, a3[:, :, 0:16], a3[:, :, 16:32], ALU.max)
        v.drain()
        v.tensor_tensor(l1[:, :, 0:8], l1[:, :, 0:8], l1[:, :, 8:16], ALU.max)
        v.drain()
        v.tensor_tensor(l1[:, :, 0:4], l1[:, :, 0:4], l1[:, :, 4:8], ALU.max)
        v.drain()
        v.tensor_tensor(l1[:, :, 0:2], l1[:, :, 0:2], l1[:, :, 2:4], ALU.max)
        v.drain()
        v.tensor_tensor(am_bf.unsqueeze(2), l1[:, :, 0:1], l1[:, :, 1:2], ALU.max)
    else:
        v.tensor_tensor(l1, a3[:, 0:16, :], a3[:, 16:32, :], ALU.max)
        v.drain()
        v.tensor_tensor(l1[:, 0:8, :], l1[:, 0:8, :], l1[:, 8:16, :], ALU.max)
        v.drain()
        v.tensor_tensor(l1[:, 0:4, :], l1[:, 0:4, :], l1[:, 4:8, :], ALU.max)
        v.drain()
        v.tensor_tensor(l1[:, 0:2, :], l1[:, 0:2, :], l1[:, 2:4, :], ALU.max)
        v.drain()
        v.tensor_tensor(am_bf.unsqueeze(1), l1[:, 0:1, :], l1[:, 1:2, :], ALU.max)
    v.drain()
    v.tensor_copy(am32, am_bf)
    v.drain()
    v.tensor_single_scalar(acl, am32, CLAMP_MIN, ALU.max)
    v.drain()
    v.tensor_single_scalar(apw.bitcast(I32), acl.bitcast(I32), EXPMASK, ALU.bitwise_and)
    v.drain()
    v.tensor_single_scalar(asc.bitcast(I32), apw.bitcast(I32), 3 << 23, ALU.subtract)
    v.drain()
    v.tensor_scalar(ain.bitcast(I32), asc.bitcast(I32), -1, 254 << 23, ALU.mult, ALU.add)
    v.drain()
    v.tensor_copy(asc16, asc)
    v.drain()
    if fuse is not None:
        srstd_col, negmr_col, rinv_b, nminv_b = fuse
        v.tensor_scalar(rinv_b, ain, srstd_col, None, ALU.mult)
        v.tensor_scalar(nminv_b, ain, negmr_col, None, ALU.mult)
        v.drain()
        extra_mul, extra_add = rinv_b, nminv_b
    if extra_mul is not None:
        v.tensor_tensor(view3_32(src32), view3_32(src32), bc32(extra_mul), ALU.mult)
        v.drain()
        v.tensor_tensor(view3_32(src32), view3_32(src32), bc32(extra_add), ALU.add)
        v.drain()
    else:
        v.tensor_tensor(view3_32(src32), view3_32(src32), bc32(ain), ALU.mult)
        v.drain()
    v.tensor_scalar(habs16, src32, MAGIC, MAGIC, ALU.add, ALU.subtract)
    v.drain()
    v.tensor_tensor(view3(dst16), view3(habs16), bc(asc16), ALU.mult)
    v.drain()


def quant_chain(v, axis, src_bf, dst_bf, view3, bc, l1, am_bf, am32, acl, apw, asc, ain,
                asc16=None, ain16=None, habs=None, pre_round_wait=None):
    if habs is None:
        habs = dst_bf
    v.tensor_single_scalar(habs.bitcast(I16), src_bf.bitcast(I16), 0x7FFF, ALU.bitwise_and)
    v.drain()
    a3 = view3(habs)
    if axis == 2:
        v.tensor_tensor(l1, a3[:, :, 0:16], a3[:, :, 16:32], ALU.max)
        v.drain()
        v.tensor_tensor(l1[:, :, 0:8], l1[:, :, 0:8], l1[:, :, 8:16], ALU.max)
        v.drain()
        v.tensor_tensor(l1[:, :, 0:4], l1[:, :, 0:4], l1[:, :, 4:8], ALU.max)
        v.drain()
        v.tensor_tensor(l1[:, :, 0:2], l1[:, :, 0:2], l1[:, :, 2:4], ALU.max)
        v.drain()
        v.tensor_tensor(am_bf.unsqueeze(2), l1[:, :, 0:1], l1[:, :, 1:2], ALU.max)
    else:
        v.tensor_tensor(l1, a3[:, 0:16, :], a3[:, 16:32, :], ALU.max)
        v.drain()
        v.tensor_tensor(l1[:, 0:8, :], l1[:, 0:8, :], l1[:, 8:16, :], ALU.max)
        v.drain()
        v.tensor_tensor(l1[:, 0:4, :], l1[:, 0:4, :], l1[:, 4:8, :], ALU.max)
        v.drain()
        v.tensor_tensor(l1[:, 0:2, :], l1[:, 0:2, :], l1[:, 2:4, :], ALU.max)
        v.drain()
        v.tensor_tensor(am_bf.unsqueeze(1), l1[:, 0:1, :], l1[:, 1:2, :], ALU.max)
    v.drain()
    v.tensor_copy(am32, am_bf)
    v.drain()
    v.tensor_single_scalar(acl, am32, CLAMP_MIN, ALU.max)
    v.drain()
    v.tensor_single_scalar(apw.bitcast(I32), acl.bitcast(I32), EXPMASK, ALU.bitwise_and)
    v.drain()
    v.tensor_single_scalar(asc.bitcast(I32), apw.bitcast(I32), 3 << 23, ALU.subtract)
    v.drain()
    v.tensor_scalar(ain.bitcast(I32), asc.bitcast(I32), -1, 254 << 23, ALU.mult, ALU.add)
    v.drain()
    v.tensor_copy(asc16, asc)
    v.tensor_copy(ain16, ain)
    v.drain()
    v.tensor_tensor(view3(habs), view3(src_bf), bc(ain16), ALU.mult)
    v.drain()
    if pre_round_wait is not None:
        pre_round_wait()
    v.tensor_scalar(src_bf, habs, MAGIC, MAGIC, ALU.add, ALU.subtract)
    v.drain()
    v.tensor_tensor(view3(dst_bf), view3(src_bf), bc(asc16), ALU.mult)
    v.drain()


def build_nc():
    nc = bass.Bass("TRN2", target_bir_lowering=False)

    x_h = nc.dram_tensor("x", [B, TLOC, D], F32, kind="ExternalInput")
    wqt_h = nc.dram_tensor("wqt", [D, M], F32, kind="ExternalInput")
    wkt_h = nc.dram_tensor("wkt", [D, M], F32, kind="ExternalInput")
    wvt_h = nc.dram_tensor("wvt", [D, M], F32, kind="ExternalInput")
    wot_h = nc.dram_tensor("wot", [M, D], F32, kind="ExternalInput")
    m0t_h = nc.dram_tensor("m0t", [M, M], F32, kind="ExternalInput")
    gate_h = nc.dram_tensor("gate", [1, 1], F32, kind="ExternalInput")
    wfc_h = nc.dram_tensor("wfc", [H, D], F32, kind="ExternalInput")
    bfc_h = nc.dram_tensor("bfc", [1, H], F32, kind="ExternalInput")
    wproj_h = nc.dram_tensor("wproj", [D, H], F32, kind="ExternalInput")
    bproj_h = nc.dram_tensor("bproj", [1, D], F32, kind="ExternalInput")
    dq_h = nc.dram_tensor("dq", [128, TOKC], F32, kind="ExternalInput")
    dk_h = nc.dram_tensor("dk", [128, TOKC], F32, kind="ExternalInput")
    mask_h = nc.dram_tensor("maskt", [128, TPC, TOKC], F16, kind="ExternalInput")
    gvt_h = nc.dram_tensor("gvt", [128, 9], F32, kind="ExternalInput")
    ones_h = nc.dram_tensor("onesr", [1, 128], F32, kind="ExternalInput")
    idn_h = nc.dram_tensor("idn", [128, 128], F16, kind="ExternalInput")
    out_h = nc.dram_tensor("out", [B, TLOC, D], F32, kind="ExternalOutput")

    zin_h = nc.dram_tensor("zin", [M, M], F16)
    zout_h = nc.dram_tensor("zout", [NCORE * M, M], F16, addr_space="Shared")

    ctx = ExitStack()
    sb = lambda n, s, d: ctx.enter_context(nc.sbuf_tensor(n, s, d))
    sem = lambda n: ctx.enter_context(nc.semaphore(n))

    with ctx:
        xqT = sb("xqT", [128, 8, NTILE * 128], F8)
        wfcT = sb("wfcT", [128, 8, H], F8)
        rT = sb("rT", [128, 2, NTILE * 128], F16)
        wot_bf = sb("wot_bf", [128, 2, D], F16)
        barena = sb("barena", [1, 5376], F16)
        sgate_col = sb("sgate_col", [128, 1], F32)
        idn_sb = sb("idn_sb", [128, 128], F16)
        eps_col = sb("eps_col", [128, 1], F32)
        bfc_r = barena[0:1, 0:H]
        bpr_r = barena[0:1, H:H + D]
        ones_r = barena[0:1, H + D:H + D + 128]
        sig_b = barena[0:1, H + D + 128:H + D + 129]

        swl = sem("swl"); sSIG = sem("sSIG"); sSGR = sem("sSGR"); sGMM = sem("sGMM")
        sWCV = sem("sWCV"); swfl = sem("swfl"); sWQD = sem("sWQD"); sWT = sem("sWT")
        sW8 = sem("sW8"); sxl = sem("sxl"); sBNA = sem("sBNA"); sRSD = sem("sRSD")
        sNM = sem("sNM"); sAPL = sem("sAPL"); sXNT = sem("sXNT"); sXQ = sem("sXQ")
        sXQT = sem("sXQT"); sXQ8 = sem("sXQ8"); sQMM = sem("sQMM"); sKMM = sem("sKMM")
        sVMM = sem("sVMM"); sQE = sem("sQE"); sKE = sem("sKE"); sVE = sem("sVE")
        sKTC = sem("sKTC"); sAMM = sem("sAMM"); sAE = sem("sAE"); sRMM = sem("sRMM")
        sRE = sem("sRE"); sUMM = sem("sUMM"); sUE = sem("sUE"); sZC = sem("sZC")
        sCCD = sem("sCCD"); sCC = sem("sCC"); sSTI = sem("sSTI"); sSTB = sem("sSTB")
        sRIM = sem("sRIM"); sRIE = sem("sRIE"); sWPL = sem("sWPL"); sPQ = sem("sPQ")
        sWPT = sem("sWPT"); sxl2 = sem("sxl2"); sH1 = sem("sH1"); sGEL = sem("sGEL")
        sHQ = sem("sHQ"); sHQT = sem("sHQT"); sPOM = sem("sPOM"); sSIL = sem("sSIL")
        sSMM = sem("sSMM"); sCMB = sem("sCMB"); sOUT = sem("sOUT")
        sXNM = sem("sXNM"); sXNE = sem("sXNE"); swlg = sem("swlg"); sTAN = sem("sTAN")
        sWP8 = sem("sWP8"); sH8 = sem("sH8")
        swfl2 = [sem("swflA"), sem("swflB")]
        sWPL2 = [sem("sWPLA"), sem("sWPLB")]
        sCCz = [sem("sCCzA"), sem("sCCzB")]
        sxlp = [sem("sxlA"), sem("sxlB")]

        with ExitStack() as w1:
            sbw = lambda n, s, d: w1.enter_context(nc.sbuf_tensor(n, s, d))
            sgin = sbw("sgin", [1, 1], F32)
            psg = w1.enter_context(nc.psum_tensor("psg", [128, 1], F32))
            blk = w1.enter_context(nc.Block("w1a"))

            @blk.gpsimd
            def _(g):
                g.dma_start(out=bfc_r, in_=bfc_h.ap()).then_inc(swlg, 16)
                g.dma_start(out=bpr_r, in_=bproj_h.ap()).then_inc(swlg, 16)
                g.dma_start(out=ones_r, in_=ones_h.ap()).then_inc(swlg, 16)

            @blk.sync
            def _(s):
                s.dma_start(out=sgin[:, :], in_=gate_h.ap()).then_inc(swl, 16)
                s.dma_start(out=idn_sb[:, :], in_=idn_h.ap()).then_inc(swl, 16)

            @blk.scalar
            def _(s):
                s.wait_ge(swl, 32)
                s.activation(sig_b, sgin[:, :], AF.Sigmoid).then_inc(sSIG, 1)

            @blk.vector
            def _(v):
                v.memset(eps_col[:, :], 1e-5)
                v.wait_ge(sGMM, 1)
                v.tensor_copy(sgate_col[:, :], psg[:, :]).then_inc(sWCV, 1)

            @blk.tensor
            def _(t):
                t.wait_ge(sSIG, 1)
                t.wait_ge(swlg, 48)
                t.matmul(psg[:, :], ones_r, sig_b, start=True, stop=True).then_inc(sGMM, 1)

        with ExitStack() as w2:
            sbw = lambda n, s, d: w2.enter_context(nc.sbuf_tensor(n, s, d))
            swf = [sbw(f"swf{i}", [128, D], F32) for i in range(2)]
            wq_bf = sbw("w_wq", [128, D], F16)
            habs = sbw("w_habs", [128, D], F16)
            l1 = sbw("w_l1", [128, 32, 16], F16)
            am_bf = sbw("w_am", [128, 32], F16)
            am32 = sbw("w_am32", [128, 32], F32)
            acl = sbw("w_acl", [128, 32], F32)
            apw = sbw("w_apw", [128, 32], F32)
            asc = sbw("w_asc", [128, 32], F32)
            ain = sbw("w_ain", [128, 32], F32)
            asc16 = sbw("w_asc16", [128, 32], F16)
            ain16 = sbw("w_ain16", [128, 32], F16)
            tst = [sbw(f"w_tst{i}", [128, 8, 128], F16) for i in range(2)]
            blk = w2.enter_context(nc.Block("w1b"))

            @blk.gpsimd
            def _(g):
                for T in range(32):
                    if T >= 2:
                        g.wait_ge(sWQD, T - 1)
                    g.dma_start(out=swf[T % 2][:, :], in_=wfc_h.ap()[128 * T:128 * (T + 1), :]).then_inc(swfl2[T % 2], 16)

            @blk.vector
            def _(v):
                view3 = lambda a: a.rearrange("p (b e) -> p b e", b=32)
                bc = lambda a: a.unsqueeze(2).broadcast_to([128, 32, 32])
                for T in range(32):
                    v.wait_ge(swfl2[T % 2], 16 * (T // 2 + 1))
                    if T >= 1:
                        v.wait_ge(sWT, 16 * T)
                    quant_f32(v, 2, swf[T % 2][:, :], wq_bf[:, :], habs[:, :], view3, bc, view3, bc,
                              l1[:, :, :], am_bf[:, :], am32[:, :], acl[:, :],
                              apw[:, :], asc[:, :], ain[:, :], asc16[:, :])
                    v.engine_nop().then_inc(sWQD, 1)
                    if T >= 1:
                        v.wait_ge(sWT, 16 * T)
                        v.tensor_copy(wfcT[:, :, 128 * (T - 1):128 * T], tst[(T - 1) % 2][:, :, :])
                        v.engine_nop().then_inc(sW8, 1)
                v.wait_ge(sWT, 16 * 32)
                v.tensor_copy(wfcT[:, :, 128 * 31:128 * 32], tst[31 % 2][:, :, :])
                v.engine_nop().then_inc(sW8, 1)

            @blk.scalar
            def _(s):
                for T in range(32):
                    s.wait_ge(sWQD, T + 1)
                    if T >= 2:
                        s.wait_ge(sW8, T - 1)
                    s.dma_start(out=tst[T % 2][:, :, :],
                                in_=wq_bf[:, :].rearrange("p (k c) -> p k c", k=8),
                                transpose=True).then_inc(sWT, 16)

        with ExitStack() as ab:
            sba = lambda n, s, d: ab.enter_context(nc.sbuf_tensor(n, s, d))
            wq_sb = sba("wq_sb", [128, 8, M], F16)
            wk_sb = sba("wk_sb", [128, 8, M], F16)
            wv_sb = sba("wv_sb", [128, 8, M], F16)
            m0_sb = sba("m0_sb", [128, 2, M], F32)
            dq_sb = sba("dq_sb", [128, TOKC], F32)
            dk_sb = sba("dk_sb", [128, TOKC], F32)
            mask_sb = sba("mask_sb", [128, TPC, TOKC], F16)
            gvt_sb = sba("gvt_sb", [128, 9], F32)
            qT_all = sba("qT_all", [128, 2, NTILE * 128], F16)
            U_all = sba("U_all", [128, 2, NCHUNK, M], F16)
            Zgt = [sba(f"Zgt{i}", [128, 2, M], F16) for i in range(2)]
            STa = sba("STa", [128, 2, M], F32)
            STb = sba("STb", [128, 2, M], F32)
            st_bf = sba("st_bf", [128, 2, M], F16)
            Za = sba("Za", [128, 2, M], F32)
            Zb = sba("Zb", [128, 2, M], F32)
            zbf = sba("zbf", [128, 2, M], F16)
            xnT = sba("xnT", [128, 8, TOKC], F16)
            ktT = sba("ktT", [128, 2, TOKC], F16)
            ktc = sba("ktc", [128, TPC, M], F16)
            v_sb = sba("v_sb", [128, TPC, M], F16)
            A_sb = sba("A_sb", [128, TPC, TOKC], F16)
            sx = [sba(f"sx{i}", [128, D], F32) for i in range(2)]
            sxn = [sba(f"sxn{i}", [128, D], F16) for i in range(2)]
            sxq = [sba(f"sxq{i}", [128, D], F16) for i in range(2)]
            xst = [sba(f"xst{i}", [128, 8, 128], F16) for i in range(2)]
            bst = sba("bst", [128, 2, 6], F32)
            bag = sba("bag", [128, 2], F32)
            slnv = sba("slnv", [128, 1], F32)
            srstd = sba("srstd", [128, 1], F32)
            negmr = sba("negmr", [128, 1], F32)
            habs = sba("a_habs", [128, D], F16)
            l1 = sba("a_l1", [128, 32, 16], F16)
            am_bf = sba("a_am", [128, 32], F16)
            am32 = sba("a_am32", [128, 32], F32)
            acl = sba("a_acl", [128, 32], F32)
            apw = sba("a_apw", [128, 32], F32)
            asc = sba("a_asc", [128, 32], F32)
            ain = sba("a_ain", [128, 32], F32)
            asc16 = sba("a_asc16", [128, 32], F16)
            ain16 = sba("a_ain16", [128, 32], F16)
            rinv = sba("a_rinv", [128, 32], F32)
            nminv = sba("a_nminv", [128, 32], F32)
            P1 = ab.enter_context(nc.psum_tensor("P1", [128, 2048], F32))
            P2 = ab.enter_context(nc.psum_tensor("P2", [128, 1024], F32))
            P3 = ab.enter_context(nc.psum_tensor("P3", [128, 512], F32))
            PT = ab.enter_context(nc.psum_tensor("PT", [128, 1024], F16))
            blk = ab.enter_context(nc.Block("phAB"))

            x_t = x_h.ap().transpose([1, 0, 2])

            @blk.sync
            def _(s):
                s.dma_start(out=m0_sb[:, :, :], in_=m0t_h.ap().rearrange("(t p) m -> p t m", p=128)).then_inc(swl, 16)
                s.dma_start(out=dq_sb[:, :], in_=dq_h.ap()).then_inc(swl, 16)
                s.dma_start(out=dk_sb[:, :], in_=dk_h.ap()).then_inc(swl, 16)
                s.dma_start(out=mask_sb[:, :, :], in_=mask_h.ap()).then_inc(swl, 16)
                s.dma_start(out=gvt_sb[:, :], in_=gvt_h.ap()).then_inc(swl, 16)
                for tt in range(NTILE):
                    if tt >= 2:
                        s.wait_ge(sXQ, tt - 1)
                    s.dma_start(out=sx[tt % 2][:, :], in_=x_t[8 * tt:8 * (tt + 1)]).then_inc(sxlp[tt % 2], 16)

            @blk.gpsimd
            def _(g):
                g.dma_start(out=wq_sb[:, :, :], in_=wqt_h.ap().rearrange("(k p) m -> p k m", p=128)).then_inc(swlg, 16)
                g.dma_start(out=wk_sb[:, :, :], in_=wkt_h.ap().rearrange("(k p) m -> p k m", p=128)).then_inc(swlg, 16)
                g.dma_start(out=wv_sb[:, :, :], in_=wvt_h.ap().rearrange("(k p) m -> p k m", p=128)).then_inc(swlg, 16)
                g.dma_start(out=wot_bf[:, :, :], in_=wot_h.ap().rearrange("(t p) h -> p t h", p=128)).then_inc(swlg, 16)
                g.wait_ge(sZC, 1)
                g.dma_start(out=zin_h.ap().rearrange("(t p) m -> p t m", p=128), in_=zbf[:, :, :]).then_inc(sCCD, 16)
                g.wait_ge(sCCD, 16)
                g.collective_compute(
                    "AllGather", ALU.bypass,
                    replica_groups=[list(range(NCORE))],
                    ins=[zin_h.ap()],
                    outs=[zout_h.ap()],
                ).then_inc(sCC)
                g.wait_ge(sCC, 1)
                for j in range(NCORE):
                    if j >= 2:
                        g.wait_ge(sSTI, j - 1)
                    g.dma_start(out=Zgt[j % 2][:, :, :],
                                in_=zout_h.ap()[M * j:M * (j + 1), :].rearrange("(t p) m -> p t m", p=128)).then_inc(sCCz[j % 2], 16)

            @blk.vector
            def _(v):
                view3 = lambda a: a.rearrange("p (b e) -> p b e", b=32)
                bc = lambda a: a.unsqueeze(2).broadcast_to([128, 32, 32])
                for c in range(NCHUNK):
                    for j in range(TPC):
                        tt = TPC * c + j
                        v.wait_ge(sxlp[tt % 2], 16 * (tt // 2 + 1))
                        if tt >= 1:
                            v.wait_ge(sRSD, tt)
                        v.bn_stats(bst[:, 0, :], sx[tt % 2][:, 0:512])
                        v.bn_stats(bst[:, 1, :], sx[tt % 2][:, 512:1024])
                        v.drain()
                        v.bn_aggr(bag[:, :], bst[:, :, :])
                        v.drain()
                        v.engine_nop().then_inc(sBNA, 1)
                        v.wait_ge(sRSD, tt + 1)
                        v.scalar_tensor_tensor(negmr[:, :], bag[:, 0:1], -1.0, srstd[:, :], ALU.mult, ALU.mult)
                        v.drain()
                        v.engine_nop().then_inc(sNM, 1)
                        v.wait_ge(sAPL, tt + 1)
                        quant_f32(v, 2, sx[tt % 2][:, :], sxq[tt % 2][:, :], habs[:, :],
                                  view3, bc, view3, bc,
                                  l1[:, :, :], am_bf[:, :], am32[:, :], acl[:, :],
                                  apw[:, :], asc[:, :], ain[:, :], asc16[:, :],
                                  amax_src16=sxn[tt % 2][:, :],
                                  fuse=(srstd[:, :], negmr[:, :], rinv[:, :], nminv[:, :]))
                        v.engine_nop().then_inc(sXQ, 1)
                        v.wait_ge(sXQT, 16 * (tt + 1))
                        v.tensor_copy(xqT[:, :, 128 * tt:128 * (tt + 1)], xst[tt % 2][:, :, :])
                        v.engine_nop().then_inc(sXQ8, 1)
                    v.wait_ge(sQMM, c + 1)
                    for mt in range(2):
                        v.tensor_tensor(qT_all[:, mt, TOKC * c:TOKC * (c + 1)],
                                        P1[:, 512 * mt:512 * (mt + 1)], dq_sb[:, :], ALU.mult)
                    v.engine_nop().then_inc(sQE, 1)
                    v.wait_ge(sKMM, c + 1)
                    for mt in range(2):
                        v.tensor_tensor(ktT[:, mt, :],
                                        P1[:, 1024 + 512 * mt:1024 + 512 * (mt + 1)], dk_sb[:, :], ALU.mult)
                    v.engine_nop().then_inc(sKE, 1)
                    v.wait_ge(sAMM, c + 1)
                    for ut in range(TPC):
                        v.tensor_tensor(A_sb[:, ut, :], P1[:, 512 * ut:512 * (ut + 1)],
                                        mask_sb[:, ut, :], ALU.mult)
                    v.engine_nop().then_inc(sAE, 1)

                v.memset(Za[:, :, :], 0.0)
                v.drain()
                for c in range(NCHUNK):
                    v.wait_ge(sUE, c + 1)
                    v.tensor_tensor(Zb[:, :, :], Za[:, :, :], U_all[:, :, c, :], ALU.add)
                    v.drain()
                    v.tensor_single_scalar(Za[:, :, :], Zb[:, :, :], DEC32, ALU.mult)
                    v.drain()
                v.tensor_copy(zbf[:, :, :], Za[:, :, :])
                v.drain()
                v.engine_nop().then_inc(sZC, 1)
                v.wait_ge(swl, 112)
                v.tensor_scalar(STa[:, :, :], m0_sb[:, :, :], gvt_sb[:, 8:9], None, ALU.mult)
                v.drain()
                for j in range(NCORE):
                    v.wait_ge(sCCz[j % 2], 16 * (j // 2 + 1))
                    v.tensor_scalar(Zb[:, :, :], Zgt[j % 2][:, :, :], gvt_sb[:, j:j + 1], None, ALU.mult)
                    v.drain()
                    v.engine_nop().then_inc(sSTI, 1)
                    v.tensor_tensor(STa[:, :, :], STa[:, :, :], Zb[:, :, :], ALU.add)
                    v.drain()
                for c in range(NCHUNK):
                    v.tensor_copy(st_bf[:, :, :], STa[:, :, :])
                    v.drain()
                    v.engine_nop().then_inc(sSTB, 1)
                    v.tensor_tensor(STb[:, :, :], STa[:, :, :], U_all[:, :, c, :], ALU.add)
                    v.drain()
                    v.tensor_single_scalar(STa[:, :, :], STb[:, :, :], DEC32, ALU.mult)
                    v.drain()
                    v.wait_ge(sRIM, c + 1)
                    for mt in range(2):
                        v.tensor_tensor(rT[:, mt, TOKC * c:TOKC * (c + 1)],
                                        rT[:, mt, TOKC * c:TOKC * (c + 1)],
                                        P2[:, 512 * mt:512 * (mt + 1)], ALU.add)
                    v.engine_nop().then_inc(sRIE, 1)

            @blk.scalar
            def _(s):
                for c in range(NCHUNK):
                    for j in range(TPC):
                        tt = TPC * c + j
                        s.wait_ge(sBNA, tt + 1)
                        s.activation(slnv[:, :], bag[:, 1:2], AF.Ln, bias=eps_col[:, :])
                        s.drain()
                        s.activation(srstd[:, :], slnv[:, :], AF.Exp, scale=-0.5)
                        s.drain()
                        s.nop().then_inc(sRSD, 1)
                        s.wait_ge(sNM, tt + 1)
                        if tt >= 2:
                            s.wait_ge(sXNM, tt - 1)
                            s.wait_ge(sXQ, tt - 1)
                        s.activation(sxn[tt % 2][:, :], sx[tt % 2][:, :], AF.Identity,
                                     bias=negmr[:, :], scale=srstd[:, :]).then_inc(sAPL, 1)
                        s.wait_ge(sXNM, tt + 1)
                        if j == 0 and c >= 1:
                            s.wait_ge(sVMM, c)
                        s.activation(xnT[:, :, 128 * j:128 * (j + 1)],
                                     PT[:, :].rearrange("p (k c2) -> p k c2", k=8),
                                     AF.Identity).then_inc(sXNE, 1)
                        s.wait_ge(sXQ, tt + 1)
                        if tt >= 2:
                            s.wait_ge(sXQ8, tt - 1)
                        s.dma_start(out=xst[tt % 2][:, :, :],
                                    in_=sxq[tt % 2][:, :].rearrange("p (k c2) -> p k c2", k=8),
                                    transpose=True).then_inc(sXQT, 16)
                    s.wait_ge(sVMM, c + 1)
                    s.activation(v_sb[:, :, :], P2[:, :].rearrange("p (a b) -> p a b", a=TPC), AF.Identity).then_inc(sVE, 1)
                    s.wait_ge(sKE, c + 1)
                    if c >= 1:
                        s.wait_ge(sUMM, c)
                    for nt in range(2):
                        s.dma_start(out=ktc[:, :, 128 * nt:128 * (nt + 1)],
                                    in_=ktT[:, nt, :].rearrange("p (u c2) -> p u c2", u=TPC),
                                    transpose=True).then_inc(sKTC, 16)
                    s.wait_ge(sRMM, c + 1)
                    s.activation(rT[:, :, TOKC * c:TOKC * (c + 1)],
                                 P2[:, :].rearrange("p (a b) -> p a b", a=2), AF.Identity).then_inc(sRE, 1)
                    s.wait_ge(sUMM, c + 1)
                    s.activation(U_all[:, :, c, :], P3[:, :].rearrange("p (a b) -> p a b", a=2), AF.Identity).then_inc(sUE, 1)

            @blk.tensor
            def _(t):
                t.wait_ge(sWCV, 1)
                t.wait_ge(swl, 112)
                t.wait_ge(swlg, 112)
                for c in range(NCHUNK):
                    for j in range(TPC):
                        tt = TPC * c + j
                        t.wait_ge(sAPL, tt + 1)
                        if tt >= 1:
                            t.wait_ge(sXNE, tt)
                        for k in range(8):
                            i = t.transpose(PT[:, 128 * k:128 * (k + 1)],
                                            sxn[tt % 2][:, 128 * k:128 * (k + 1)], idn_sb[:, :])
                        i.then_inc(sXNM, 1)
                    t.wait_ge(sXNE, TPC * (c + 1))
                    if c >= 1:
                        t.wait_ge(sAE, c)
                        t.wait_ge(sKE, c)
                    for mt in range(2):
                        for k in range(8):
                            i = t.matmul(P1[:, 512 * mt:512 * (mt + 1)], wq_sb[:, k, 128 * mt:128 * (mt + 1)],
                                         xnT[:, k, :], start=(k == 0), stop=(k == 7))
                    i.then_inc(sQMM, 1)
                    for mt in range(2):
                        for k in range(8):
                            i = t.matmul(P1[:, 1024 + 512 * mt:1024 + 512 * (mt + 1)], wk_sb[:, k, 128 * mt:128 * (mt + 1)],
                                         xnT[:, k, :], start=(k == 0), stop=(k == 7))
                    i.then_inc(sKMM, 1)
                    if c >= 1:
                        t.wait_ge(sRE, c)
                    for vt in range(TPC):
                        for k in range(8):
                            i = t.matmul(P2[:, 256 * vt:256 * (vt + 1)], xnT[:, k, 128 * vt:128 * (vt + 1)],
                                         wv_sb[:, k, :], start=(k == 0), stop=(k == 7))
                    i.then_inc(sVMM, 1)
                    t.wait_ge(sQE, c + 1)
                    t.wait_ge(sKE, c + 1)
                    for ut in range(TPC):
                        for km in range(2):
                            i = t.matmul(P1[:, 512 * ut:512 * (ut + 1)], ktT[:, km, 128 * ut:128 * (ut + 1)],
                                         qT_all[:, km, TOKC * c:TOKC * (c + 1)], start=(km == 0), stop=(km == 1))
                    i.then_inc(sAMM, 1)
                    t.wait_ge(sAE, c + 1)
                    t.wait_ge(sVE, c + 1)
                    for mt in range(2):
                        for ut in range(TPC):
                            i = t.matmul(P2[:, 512 * mt:512 * (mt + 1)], v_sb[:, ut, 128 * mt:128 * (mt + 1)],
                                         A_sb[:, ut, :], start=(ut == 0), stop=(ut == 3))
                    i.then_inc(sRMM, 1)
                    t.wait_ge(sKTC, 32 * (c + 1))
                    if c >= 1:
                        t.wait_ge(sUE, c)
                    for nt in range(2):
                        for ut in range(TPC):
                            i = t.matmul(P3[:, 256 * nt:256 * (nt + 1)], ktc[:, ut, 128 * nt:128 * (nt + 1)],
                                         v_sb[:, ut, :], start=(ut == 0), stop=(ut == 3))
                    i.then_inc(sUMM, 1)
                for c in range(NCHUNK):
                    t.wait_ge(sSTB, c + 1)
                    if c == 0:
                        t.wait_ge(sRE, NCHUNK)
                    else:
                        t.wait_ge(sRIE, c)
                    for mt in range(2):
                        for nt in range(2):
                            i = t.matmul(P2[:, 512 * mt:512 * (mt + 1)], st_bf[:, nt, 128 * mt:128 * (mt + 1)],
                                         qT_all[:, nt, TOKC * c:TOKC * (c + 1)], start=(nt == 0), stop=(nt == 1))
                    i.then_inc(sRIM, 1)

        wprojT = sb("wprojT", [128, 32, D], F8)
        with ExitStack() as c0:
            sbc = lambda n, s, d: c0.enter_context(nc.sbuf_tensor(n, s, d))
            swp = [sbc("swp0", [128, H], F32)]
            wq_bf = sbc("c_wq", [128, H], F16)
            habs = sbc("c_habs", [128, H], F16)
            l1 = sbc("c_l1", [128, 16, 128], F16)
            am_bf = sbc("c_am", [128, 128], F16)
            am32 = sbc("c_am32", [128, 128], F32)
            acl = sbc("c_acl", [128, 128], F32)
            apw = sbc("c_apw", [128, 128], F32)
            asc = sbc("c_asc", [128, 128], F32)
            ain = sbc("c_ain", [128, 128], F32)
            asc16 = sbc("c_asc16", [128, 128], F16)
            ain16 = sbc("c_ain16", [128, 128], F16)
            wpst = [sbc(f"c_wpst{i}", [128, 32, 128], F16) for i in range(2)]
            blk = c0.enter_context(nc.Block("c0"))

            @blk.gpsimd
            def _(g):
                for T in range(8):
                    if T >= 1:
                        g.wait_ge(sPQ, T)
                    g.dma_start(out=swp[0][:, :], in_=wproj_h.ap()[128 * T:128 * (T + 1), :]).then_inc(sWPL2[T % 2], 16)

            @blk.vector
            def _(v):
                view3 = lambda a: a.rearrange("p (b e) -> p b e", b=32)
                bc = lambda a: a.unsqueeze(1).broadcast_to([128, 32, 128])
                bc32 = bc
                for T in range(8):
                    v.wait_ge(sWPL2[T % 2], 16 * (T // 2 + 1))
                    if T >= 1:
                        v.wait_ge(sWPT, 16 * T)
                    quant_f32(v, 1, swp[0][:, :], wq_bf[:, :], habs[:, :], view3, bc, view3, bc32,
                              l1[:, :, :], am_bf[:, :], am32[:, :], acl[:, :],
                              apw[:, :], asc[:, :], ain[:, :], asc16[:, :])
                    v.engine_nop().then_inc(sPQ, 1)

            @blk.scalar
            def _(s):
                for T in range(8):
                    s.wait_ge(sPQ, T + 1)
                    if T >= 2:
                        s.wait_ge(sWP8, T - 1)
                    s.dma_start(out=wpst[T % 2][:, :, :],
                                in_=wq_bf[:, :].rearrange("p (k c2) -> p k c2", k=32),
                                transpose=True).then_inc(sWPT, 16)
                    s.wait_ge(sWPT, 16 * (T + 1))
                    s.activation(wprojT[:, :, 128 * T:128 * (T + 1)], wpst[T % 2][:, :, :],
                                 AF.Identity).then_inc(sWP8, 1)

        with ExitStack() as c1:
            sbc = lambda n, s, d: c1.enter_context(nc.sbuf_tensor(n, s, d))
            h_bf = sbc("h_bf", [128, H], F16)
            hb2 = sbc("hb2", [128, H], F16)
            l1 = sbc("f_l1", [128, 16, 128], F16)
            am_bf = sbc("f_am", [128, 128], F16)
            am32 = sbc("f_am32", [128, 128], F32)
            acl = sbc("f_acl", [128, 128], F32)
            apw = sbc("f_apw", [128, 128], F32)
            asc = sbc("f_asc", [128, 128], F32)
            ain = sbc("f_ain", [128, 128], F32)
            asc16 = sbc("f_asc16", [128, 128], F16)
            ain16 = sbc("f_ain16", [128, 128], F16)
            hstage = sbc("hstage", [128, 32, 128], F16)
            hstage8 = sbc("hstage8", [128, 32, 128], F8)
            tnh = sbc("tnh", [128, D], F16)
            g_bf = sbc("g_bf", [128, D], F16)
            sx2 = sbc("sx2", [128, D], F32)
            sout = sbc("sout", [128, D], F32)
            PH = c1.enter_context(nc.psum_tensor("PH", [128, 2048], F32))
            PH2 = c1.enter_context(nc.psum_tensor("PH2", [128, 2048], F32))
            blk = c1.enter_context(nc.Block("c1"))
            PO = PH2[:, 0:1024]
            PS2 = PH2[:, 1024:2048]

            x_t = x_h.ap().transpose([1, 0, 2])
            o_t = out_h.ap().transpose([1, 0, 2])

            def mm1_half(t, tile, half):
                for k in range(4):
                    for jj in range(4):
                        t.matmul(PH[:, 512 * jj:512 * (jj + 1)],
                                 xqT[:, 2 * k:2 * k + 2, 128 * tile:128 * (tile + 1)],
                                 wfcT[:, 2 * k:2 * k + 2, 2048 * half + 512 * jj:2048 * half + 512 * (jj + 1)],
                                 start=(k == 0), stop=False, perf_mode=PM)
                for jj in range(4):
                    i = t.matmul(PH[:, 512 * jj:512 * (jj + 1)], ones_r,
                                 bfc_r[:, 2048 * half + 512 * jj:2048 * half + 512 * (jj + 1)],
                                 start=False, stop=True)
                i.then_inc(sH1, 1)

            def mm2_smm(t, u):
                if u >= 1:
                    t.wait_ge(sCMB, u)
                for nt in range(2):
                    for mt in range(2):
                        i = t.matmul(PS2[:, 512 * nt:512 * (nt + 1)],
                                     rT[:, mt, 128 * u:128 * (u + 1)],
                                     wot_bf[:, mt, 512 * nt:512 * (nt + 1)],
                                     start=(mt == 0), stop=(mt == 1))
                i.then_inc(sSMM, 1)
                t.wait_ge(sH8, u + 1)
                if u >= 1:
                    t.wait_ge(sSIL, u)
                for k in range(16):
                    for nt in range(2):
                        t.matmul(PO[:, 512 * nt:512 * (nt + 1)], hstage8[:, 2 * k:2 * k + 2, :],
                                 wprojT[:, 2 * k:2 * k + 2, 512 * nt:512 * (nt + 1)],
                                 start=(k == 0), stop=False, perf_mode=PM)
                for nt in range(2):
                    i = t.matmul(PO[:, 512 * nt:512 * (nt + 1)], ones_r,
                                 bpr_r[:, 512 * nt:512 * (nt + 1)], start=False, stop=True)
                i.then_inc(sPOM, 1)

            @blk.sync
            def _(s):
                for tt in range(NTILE):
                    if tt >= 1:
                        s.wait_ge(sCMB, tt)
                    s.dma_start(out=sx2[:, :], in_=x_t[8 * tt:8 * (tt + 1)]).then_inc(sxl2, 16)
                    s.wait_ge(sHQ, tt + 1)
                    if tt >= 1:
                        s.wait_ge(sH8, tt)
                    s.dma_start(out=hstage[:, :, :],
                                in_=hb2[:, :].rearrange("p (k c2) -> p k c2", k=32),
                                transpose=True).then_inc(sHQT, 16)
                    if tt >= 1:
                        s.wait_ge(sCMB, tt)
                        s.dma_start(out=o_t[8 * (tt - 1):8 * tt], in_=sout[:, :]).then_inc(sOUT, 16)
                s.wait_ge(sCMB, NTILE)
                s.dma_start(out=o_t[8 * (NTILE - 1):8 * NTILE], in_=sout[:, :]).then_inc(sOUT, 16)
                s.wait_ge(sOUT, 16 * NTILE)

            @blk.tensor
            def _(t):
                t.wait_ge(sW8, 32)
                t.wait_ge(sXQ8, NTILE)
                t.wait_ge(sWP8, 8)
                t.wait_ge(swlg, 112)
                for tt in range(NTILE + 1):
                    if tt < NTILE:
                        t.wait_ge(sGEL, 2 * tt)
                        mm1_half(t, tt, 0)
                    if tt >= 1:
                        mm2_smm(t, tt - 1)
                    if tt < NTILE:
                        t.wait_ge(sGEL, 2 * tt + 1)
                        mm1_half(t, tt, 1)

            @blk.scalar
            def _(s):
                for tt in range(NTILE):
                    s.wait_ge(sH1, 2 * tt + 1)
                    if tt >= 1:
                        s.wait_ge(sHQ, tt)
                    s.activation(h_bf[:, 0:2048], PH[:, :], AF.Gelu).then_inc(sGEL, 1)
                    if tt >= 1:
                        s.wait_ge(sHQT, 16 * tt)
                        if tt >= 2:
                            s.wait_ge(sPOM, tt - 1)
                        s.activation(hstage8[:, :, :], hstage[:, :, :], AF.Identity).then_inc(sH8, 1)
                        s.wait_ge(sPOM, tt)
                        s.activation(tnh[:, :], PO, AF.Tanh, scale=0.5).then_inc(sTAN, 1)
                    s.wait_ge(sH1, 2 * tt + 2)
                    s.activation(h_bf[:, 2048:4096], PH[:, :], AF.Gelu).then_inc(sGEL, 1)
                s.wait_ge(sHQT, 16 * NTILE)
                s.wait_ge(sPOM, NTILE - 1)
                s.activation(hstage8[:, :, :], hstage[:, :, :], AF.Identity).then_inc(sH8, 1)
                s.wait_ge(sPOM, NTILE)
                s.activation(tnh[:, :], PO, AF.Tanh, scale=0.5).then_inc(sTAN, 1)

            @blk.vector
            def _(v):
                view3 = lambda a: a.rearrange("p (b e) -> p b e", b=32)
                bc = lambda a: a.unsqueeze(1).broadcast_to([128, 32, 128])
                for tt in range(NTILE + 1):
                    if tt >= 1:
                        u = tt - 1
                        v.wait_ge(sTAN, u + 1)
                        v.tensor_scalar(tnh[:, :], tnh[:, :], 0.5, 0.5, ALU.mult, ALU.add)
                        v.drain()
                        v.tensor_tensor(g_bf[:, :], tnh[:, :], PO, ALU.mult)
                        v.drain()
                        v.engine_nop().then_inc(sSIL, 1)
                        v.wait_ge(sSMM, u + 1)
                        v.wait_ge(sxl2, 16 * (u + 1))
                        if u >= 1:
                            v.wait_ge(sOUT, 16 * u)
                        v.scalar_tensor_tensor(sout[:, :], PS2, sgate_col[:, :], g_bf[:, :], ALU.mult, ALU.mult)
                        v.drain()
                        v.tensor_tensor(sout[:, :], sout[:, :], sx2[:, :], ALU.add)
                        v.engine_nop().then_inc(sCMB, 1)
                    if tt < NTILE:
                        v.wait_ge(sGEL, 2 * tt + 2)
                        if tt >= 1:
                            v.wait_ge(sHQT, 16 * tt)
                        quant_chain(v, 1, h_bf[:, :], hb2[:, :], view3, bc,
                                    l1[:, :, :], am_bf[:, :], am32[:, :], acl[:, :],
                                    apw[:, :], asc[:, :], ain[:, :], asc16[:, :], ain16[:, :], habs=None)
                        v.engine_nop().then_inc(sHQ, 1)

    return nc


_PERM = None
_NC = None


def kernel(L0_attn_x, L0_attn_Wq, L0_attn_Wk, L0_attn_Wv, L0_attn_Wo, L0_attn_gate,
           L0_attn_M0, L0_ffn_w_fc, L0_ffn_b_fc, L0_ffn_w_proj, L0_ffn_b_proj):
    global _PERM, _NC
    if _PERM is None:
        p = np.arange(H)
        _PERM = 32 * (p % 128) + p // 128
    perm = _PERM
    x = np.ascontiguousarray(L0_attn_x)

    wqt = np.ascontiguousarray(np.asarray(L0_attn_Wq).T)
    wkt = np.ascontiguousarray(np.asarray(L0_attn_Wk).T)
    wvt = np.ascontiguousarray(np.asarray(L0_attn_Wv).T)
    wot = np.ascontiguousarray(np.asarray(L0_attn_Wo).T)
    m0t = np.ascontiguousarray(np.asarray(L0_attn_M0).T)
    gate = np.ascontiguousarray(np.asarray(L0_attn_gate).reshape(1, 1))
    wfc = np.ascontiguousarray(np.asarray(L0_ffn_w_fc)[perm, :])
    bfc = np.ascontiguousarray(np.asarray(L0_ffn_b_fc)[perm].reshape(1, H))
    wproj = np.ascontiguousarray(np.asarray(L0_ffn_w_proj)[:, perm])
    bproj = np.ascontiguousarray(np.asarray(L0_ffn_b_proj).reshape(1, D))

    tl = (np.arange(TOKC) // B).astype(np.float64)
    dq = np.ascontiguousarray(np.broadcast_to(
        (np.float32(DEC) ** tl).astype(np.float32), (128, TOKC)))
    dk = np.ascontiguousarray(np.broadcast_to(
        ((np.float32(DEC) ** (-1.0 - tl)) / np.float32(B)).astype(np.float32), (128, TOKC)))
    maskt = np.zeros((128, TPC, TOKC), dtype=np.float16)
    for ut in range(TPC):
        uu = ut * 128 + np.arange(128)
        maskt[:, ut, :] = (tl[uu][:, None] < tl[None, :]).astype(np.float16)
    ones = np.ones((1, 128), np.float32)
    idn = np.eye(128, dtype=np.float16)

    if _NC is None:
        _NC = build_nc()

    in_maps = []
    for i in range(NCORE):
        gv = np.zeros((9,), np.float32)
        for j in range(i):
            gv[j] = np.float32(DEC) ** np.float32(256 * (i - 1 - j))
        gv[8] = np.float32(DEC) ** np.float32(256 * i)
        gvt = np.ascontiguousarray(np.broadcast_to(gv, (128, 9)))
        in_maps.append({
            "x": np.ascontiguousarray(x[:, TLOC * i:TLOC * (i + 1), :]),
            "wqt": wqt, "wkt": wkt, "wvt": wvt, "wot": wot, "m0t": m0t, "gate": gate,
            "wfc": wfc, "bfc": bfc, "wproj": wproj, "bproj": bproj,
            "dq": dq, "dk": dk, "maskt": maskt, "gvt": gvt, "onesr": ones, "idn": idn,
        })
    res = run_bass_kernel_spmd(_NC, in_maps, core_ids=list(range(NCORE)))
    return np.concatenate([np.asarray(res.results[i]["out"]) for i in range(NCORE)], axis=1)


# version 09
# speedup vs baseline: 1.4052x; 1.1443x over previous
import numpy as np
import ml_dtypes
import concourse.bass as bass
import concourse.mybir as mybir
from concourse.bass_utils import run_bass_kernel_spmd
from contextlib import ExitStack

F32 = mybir.dt.float32
F32R = mybir.dt.float32r
BF16 = mybir.dt.bfloat16
F16 = mybir.dt.float16
F8 = mybir.dt.float8e4
PM = mybir.MatmulPerfMode.DoubleRow
I32 = mybir.dt.int32
I16 = mybir.dt.int16
AF = mybir.ActivationFunctionType
ALU = mybir.AluOpType

NCORE = 8
B = 16
TLOC = 256
D = 1024
M = 256
H = 4096
CH = 32
NCHUNK = TLOC // CH
TOKC = CH * B
NTILE = (TLOC * B) // 128
TPC = NTILE // NCHUNK
DEC = 0.99
DEC32 = float(np.float32(DEC) ** np.float32(32))
MAGIC = 12582912.0
EXPMASK = -8388608
CLAMP_MIN = float(2.0 ** -10)


def quant_f32(v, axis, src32, dst16, habs16, view3, bc, view3_32, bc32, l1, am_bf, am32,
              acl, apw, asc, ain, asc16, extra_mul=None, extra_add=None,
              amax_src16=None, fuse=None):
    if amax_src16 is not None:
        v.tensor_single_scalar(habs16.bitcast(I16), amax_src16.bitcast(I16), 0x7FFF, ALU.bitwise_and)
    else:
        v.scalar_tensor_tensor(habs16, src32, -1.0, src32, ALU.mult, ALU.max)
    v.drain()
    a3 = view3(habs16)
    if axis == 2:
        v.tensor_tensor(l1, a3[:, :, 0:16], a3[:, :, 16:32], ALU.max)
        v.drain()
        v.tensor_tensor(l1[:, :, 0:8], l1[:, :, 0:8], l1[:, :, 8:16], ALU.max)
        v.drain()
        v.tensor_tensor(l1[:, :, 0:4], l1[:, :, 0:4], l1[:, :, 4:8], ALU.max)
        v.drain()
        v.tensor_tensor(l1[:, :, 0:2], l1[:, :, 0:2], l1[:, :, 2:4], ALU.max)
        v.drain()
        v.tensor_tensor(am_bf.unsqueeze(2), l1[:, :, 0:1], l1[:, :, 1:2], ALU.max)
    else:
        v.tensor_tensor(l1, a3[:, 0:16, :], a3[:, 16:32, :], ALU.max)
        v.drain()
        v.tensor_tensor(l1[:, 0:8, :], l1[:, 0:8, :], l1[:, 8:16, :], ALU.max)
        v.drain()
        v.tensor_tensor(l1[:, 0:4, :], l1[:, 0:4, :], l1[:, 4:8, :], ALU.max)
        v.drain()
        v.tensor_tensor(l1[:, 0:2, :], l1[:, 0:2, :], l1[:, 2:4, :], ALU.max)
        v.drain()
        v.tensor_tensor(am_bf.unsqueeze(1), l1[:, 0:1, :], l1[:, 1:2, :], ALU.max)
    v.drain()
    v.tensor_copy(am32, am_bf)
    v.drain()
    v.tensor_single_scalar(acl, am32, CLAMP_MIN, ALU.max)
    v.drain()
    v.tensor_single_scalar(apw.bitcast(I32), acl.bitcast(I32), EXPMASK, ALU.bitwise_and)
    v.drain()
    v.tensor_single_scalar(asc.bitcast(I32), apw.bitcast(I32), 3 << 23, ALU.subtract)
    v.drain()
    v.tensor_scalar(ain.bitcast(I32), asc.bitcast(I32), -1, 254 << 23, ALU.mult, ALU.add)
    v.drain()
    v.tensor_copy(asc16, asc)
    v.drain()
    if fuse is not None:
        srstd_col, negmr_col, rinv_b, nminv_b = fuse
        v.tensor_scalar(rinv_b, ain, srstd_col, None, ALU.mult)
        v.tensor_scalar(nminv_b, ain, negmr_col, None, ALU.mult)
        v.drain()
        extra_mul, extra_add = rinv_b, nminv_b
    if extra_mul is not None:
        v.tensor_tensor(view3_32(src32), view3_32(src32), bc32(extra_mul), ALU.mult)
        v.drain()
        v.tensor_tensor(view3_32(src32), view3_32(src32), bc32(extra_add), ALU.add)
        v.drain()
    else:
        v.tensor_tensor(view3_32(src32), view3_32(src32), bc32(ain), ALU.mult)
        v.drain()
    v.tensor_scalar(habs16, src32, MAGIC, MAGIC, ALU.add, ALU.subtract)
    v.drain()
    v.tensor_tensor(view3(dst16), view3(habs16), bc(asc16), ALU.mult)
    v.drain()


def quant_chain(v, axis, src_bf, dst_bf, view3, bc, l1, am_bf, am32, acl, apw, asc, ain,
                asc16=None, ain16=None, habs=None, pre_round_wait=None):
    if habs is None:
        habs = dst_bf
    v.tensor_single_scalar(habs.bitcast(I16), src_bf.bitcast(I16), 0x7FFF, ALU.bitwise_and)
    v.drain()
    a3 = view3(habs)
    if axis == 2:
        v.tensor_tensor(l1, a3[:, :, 0:16], a3[:, :, 16:32], ALU.max)
        v.drain()
        v.tensor_tensor(l1[:, :, 0:8], l1[:, :, 0:8], l1[:, :, 8:16], ALU.max)
        v.drain()
        v.tensor_tensor(l1[:, :, 0:4], l1[:, :, 0:4], l1[:, :, 4:8], ALU.max)
        v.drain()
        v.tensor_tensor(l1[:, :, 0:2], l1[:, :, 0:2], l1[:, :, 2:4], ALU.max)
        v.drain()
        v.tensor_tensor(am_bf.unsqueeze(2), l1[:, :, 0:1], l1[:, :, 1:2], ALU.max)
    else:
        v.tensor_tensor(l1, a3[:, 0:16, :], a3[:, 16:32, :], ALU.max)
        v.drain()
        v.tensor_tensor(l1[:, 0:8, :], l1[:, 0:8, :], l1[:, 8:16, :], ALU.max)
        v.drain()
        v.tensor_tensor(l1[:, 0:4, :], l1[:, 0:4, :], l1[:, 4:8, :], ALU.max)
        v.drain()
        v.tensor_tensor(l1[:, 0:2, :], l1[:, 0:2, :], l1[:, 2:4, :], ALU.max)
        v.drain()
        v.tensor_tensor(am_bf.unsqueeze(1), l1[:, 0:1, :], l1[:, 1:2, :], ALU.max)
    v.drain()
    v.tensor_copy(am32, am_bf)
    v.drain()
    v.tensor_single_scalar(acl, am32, CLAMP_MIN, ALU.max)
    v.drain()
    v.tensor_single_scalar(apw.bitcast(I32), acl.bitcast(I32), EXPMASK, ALU.bitwise_and)
    v.drain()
    v.tensor_single_scalar(asc.bitcast(I32), apw.bitcast(I32), 3 << 23, ALU.subtract)
    v.drain()
    v.tensor_scalar(ain.bitcast(I32), asc.bitcast(I32), -1, 254 << 23, ALU.mult, ALU.add)
    v.drain()
    v.tensor_copy(asc16, asc)
    v.tensor_copy(ain16, ain)
    v.drain()
    v.tensor_tensor(view3(habs), view3(src_bf), bc(ain16), ALU.mult)
    v.drain()
    if pre_round_wait is not None:
        pre_round_wait()
    v.tensor_scalar(src_bf, habs, MAGIC, MAGIC, ALU.add, ALU.subtract)
    v.drain()
    v.tensor_tensor(view3(dst_bf), view3(src_bf), bc(asc16), ALU.mult)
    v.drain()


def build_nc():
    nc = bass.Bass("TRN2", target_bir_lowering=False)

    x_h = nc.dram_tensor("x", [B, TLOC, D], F32, kind="ExternalInput")
    wqt_h = nc.dram_tensor("wqt", [D, M], F32, kind="ExternalInput")
    wkt_h = nc.dram_tensor("wkt", [D, M], F32, kind="ExternalInput")
    wvt_h = nc.dram_tensor("wvt", [D, M], F32, kind="ExternalInput")
    wot_h = nc.dram_tensor("wot", [M, D], F32, kind="ExternalInput")
    m0t_h = nc.dram_tensor("m0t", [M, M], F32, kind="ExternalInput")
    gate_h = nc.dram_tensor("gate", [1, 1], F32, kind="ExternalInput")
    wfc_h = nc.dram_tensor("wfc", [H, D], F32, kind="ExternalInput")
    bfc_h = nc.dram_tensor("bfc", [1, H], F32, kind="ExternalInput")
    wproj_h = nc.dram_tensor("wproj", [D, H], F32, kind="ExternalInput")
    bproj_h = nc.dram_tensor("bproj", [1, D], F32, kind="ExternalInput")
    dq_h = nc.dram_tensor("dq", [128, TOKC], F16, kind="ExternalInput")
    dk_h = nc.dram_tensor("dk", [128, TOKC], F16, kind="ExternalInput")
    mask_h = nc.dram_tensor("maskt", [128, TPC, TOKC], F16, kind="ExternalInput")
    gvt_h = nc.dram_tensor("gvt", [128, 9], F32, kind="ExternalInput")
    ones_h = nc.dram_tensor("onesr", [1, 128], F32, kind="ExternalInput")
    idn_h = nc.dram_tensor("idn", [128, 128], F16, kind="ExternalInput")
    out_h = nc.dram_tensor("out", [B, TLOC, D], F32, kind="ExternalOutput")

    zin_h = nc.dram_tensor("zin", [M, M], F16)
    zout_h = nc.dram_tensor("zout", [NCORE * M, M], F16, addr_space="Shared")

    ctx = ExitStack()
    sb = lambda n, s, d: ctx.enter_context(nc.sbuf_tensor(n, s, d))
    sem = lambda n: ctx.enter_context(nc.semaphore(n))

    with ctx:
        xqT = sb("xqT", [128, 8, NTILE * 128], F8)
        wfcT = sb("wfcT", [128, 8, H], F8)
        rT = sb("rT", [128, 2, NTILE * 128], F16)
        wot_bf = sb("wot_bf", [128, 2, D], F16)
        barena = sb("barena", [1, 5376], F16)
        sgate_col = sb("sgate_col", [128, 1], F32)
        idn_sb = sb("idn_sb", [128, 128], F16)
        eps_col = sb("eps_col", [128, 1], F32)
        bfc_r = barena[0:1, 0:H]
        bpr_r = barena[0:1, H:H + D]
        ones_r = barena[0:1, H + D:H + D + 128]
        sig_b = barena[0:1, H + D + 128:H + D + 129]

        swl = sem("swl"); sSIG = sem("sSIG"); sSGR = sem("sSGR"); sGMM = sem("sGMM")
        sWCV = sem("sWCV"); swfl = sem("swfl"); sWQD = sem("sWQD"); sWT = sem("sWT")
        sW8 = sem("sW8"); sxl = sem("sxl"); sBNA = sem("sBNA"); sRSD = sem("sRSD")
        sNM = sem("sNM"); sAPL = sem("sAPL"); sXNT = sem("sXNT"); sXQ = sem("sXQ")
        sXQT = sem("sXQT"); sXQ8 = sem("sXQ8"); sQMM = sem("sQMM"); sKMM = sem("sKMM")
        sVMM = sem("sVMM"); sQE = sem("sQE"); sKE = sem("sKE"); sVE = sem("sVE")
        sKTC = sem("sKTC"); sAMM = sem("sAMM"); sAE = sem("sAE"); sRMM = sem("sRMM")
        sRE = sem("sRE"); sUMM = sem("sUMM"); sUE = sem("sUE"); sZC = sem("sZC")
        sCCD = sem("sCCD"); sCC = sem("sCC"); sSTI = sem("sSTI"); sSTB = sem("sSTB")
        sRIM = sem("sRIM"); sRIE = sem("sRIE"); sWPL = sem("sWPL"); sPQ = sem("sPQ")
        sWPT = sem("sWPT"); sxl2 = sem("sxl2"); sH1 = sem("sH1"); sGEL = sem("sGEL")
        sHQ = sem("sHQ"); sHQT = sem("sHQT"); sPOM = sem("sPOM"); sSIL = sem("sSIL")
        sSMM = sem("sSMM"); sCMB = sem("sCMB"); sOUT = sem("sOUT")
        sXNM = sem("sXNM"); sXNE = sem("sXNE"); swlg = sem("swlg"); sTAN = sem("sTAN")
        sWP8 = sem("sWP8"); sH8 = sem("sH8")
        swfl2 = [sem("swflA"), sem("swflB")]
        sWPL2 = [sem("sWPLA"), sem("sWPLB")]
        sCCz = [sem("sCCzA"), sem("sCCzB")]
        sxlp = [sem("sxlA"), sem("sxlB")]

        with ExitStack() as w1:
            sbw = lambda n, s, d: w1.enter_context(nc.sbuf_tensor(n, s, d))
            sgin = sbw("sgin", [1, 1], F32)
            psg = w1.enter_context(nc.psum_tensor("psg", [128, 1], F32))
            blk = w1.enter_context(nc.Block("w1a"))

            @blk.gpsimd
            def _(g):
                g.dma_start(out=bfc_r, in_=bfc_h.ap()).then_inc(swlg, 16)
                g.dma_start(out=bpr_r, in_=bproj_h.ap()).then_inc(swlg, 16)
                g.dma_start(out=ones_r, in_=ones_h.ap()).then_inc(swlg, 16)

            @blk.sync
            def _(s):
                s.dma_start(out=sgin[:, :], in_=gate_h.ap()).then_inc(swl, 16)
                s.dma_start(out=idn_sb[:, :], in_=idn_h.ap()).then_inc(swl, 16)

            @blk.scalar
            def _(s):
                s.wait_ge(swl, 32)
                s.activation(sig_b, sgin[:, :], AF.Sigmoid).then_inc(sSIG, 1)

            @blk.vector
            def _(v):
                v.memset(eps_col[:, :], 1e-5)
                v.wait_ge(sGMM, 1)
                v.tensor_copy(sgate_col[:, :], psg[:, :]).then_inc(sWCV, 1)

            @blk.tensor
            def _(t):
                t.wait_ge(sSIG, 1)
                t.wait_ge(swlg, 48)
                t.matmul(psg[:, :], ones_r, sig_b, start=True, stop=True).then_inc(sGMM, 1)

        with ExitStack() as ab:
            sba = lambda n, s, d: ab.enter_context(nc.sbuf_tensor(n, s, d))
            wq_sb = sba("wq_sb", [128, 8, M], F16)
            wk_sb = sba("wk_sb", [128, 8, M], F16)
            wv_sb = sba("wv_sb", [128, 8, M], F16)
            m0_sb = sba("m0_sb", [128, 2, M], F32)
            dq_sb = sba("dq_sb", [128, TOKC], F16)
            dk_sb = sba("dk_sb", [128, TOKC], F16)
            mask_sb = sba("mask_sb", [128, TPC, TOKC], F16)
            gvt_sb = sba("gvt_sb", [128, 9], F32)
            qT_all = sba("qT_all", [128, 2, NTILE * 128], F16)
            U_all = sba("U_all", [128, 2, NCHUNK, M], F16)
            Zgt = [sba(f"Zgt{i}", [128, 2, M], F16) for i in range(2)]
            STa = sba("STa", [128, 2, M], F32)
            STb = sba("STb", [128, 2, M], F32)
            st_bf = sba("st_bf", [128, 2, M], F16)
            Za = sba("Za", [128, 2, M], F32)
            Zb = sba("Zb", [128, 2, M], F32)
            zbf = sba("zbf", [128, 2, M], F16)
            xnT = sba("xnT", [128, 8, TOKC], F16)
            ktT = sba("ktT", [128, 2, TOKC], F16)
            ktc = sba("ktc", [128, TPC, M], F16)
            v_sb = sba("v_sb", [128, TPC, M], F16)
            A_sb = sba("A_sb", [128, TPC, TOKC], F16)
            sx = [sba(f"sx{i}", [128, D], F32) for i in range(2)]
            sxn = [sba(f"sxn{i}", [128, D], F16) for i in range(2)]
            sxq = [sba(f"sxq{i}", [128, D], F16) for i in range(2)]
            xst = [sba(f"xst{i}", [128, 8, 128], F16) for i in range(2)]
            bst = sba("bst", [128, 2, 6], F32)
            bag = sba("bag", [128, 2], F32)
            slnv = sba("slnv", [128, 1], F32)
            srstd = sba("srstd", [128, 1], F32)
            negmr = sba("negmr", [128, 1], F32)
            habs = sba("a_habs", [128, D], F16)
            l1 = sba("a_l1", [128, 32, 16], F16)
            am_bf = sba("a_am", [128, 32], F16)
            am32 = sba("a_am32", [128, 32], F32)
            acl = sba("a_acl", [128, 32], F32)
            apw = sba("a_apw", [128, 32], F32)
            asc = sba("a_asc", [128, 32], F32)
            ain = sba("a_ain", [128, 32], F32)
            asc16 = sba("a_asc16", [128, 32], F16)
            ain16 = sba("a_ain16", [128, 32], F16)
            rinv = sba("a_rinv", [128, 32], F32)
            swf = [sba(f"swf{i}", [128, D], F32) for i in range(2)]
            wq_bf = sba("w_wq", [128, D], F16)
            tst = [sba(f"w_tst{i}", [128, 8, 128], F16) for i in range(2)]
            nminv = sba("a_nminv", [128, 32], F32)
            P1 = ab.enter_context(nc.psum_tensor("P1", [128, 2048], F32))
            P2 = ab.enter_context(nc.psum_tensor("P2", [128, 1024], F32))
            P3 = ab.enter_context(nc.psum_tensor("P3", [128, 512], F32))
            PT = ab.enter_context(nc.psum_tensor("PT", [128, 1024], F16))
            blk = ab.enter_context(nc.Block("phAB"))

            x_t = x_h.ap().transpose([1, 0, 2])

            @blk.sync
            def _(s):
                s.dma_start(out=m0_sb[:, :, :], in_=m0t_h.ap().rearrange("(t p) m -> p t m", p=128)).then_inc(swl, 16)
                s.dma_start(out=dq_sb[:, :], in_=dq_h.ap()).then_inc(swl, 16)
                s.dma_start(out=dk_sb[:, :], in_=dk_h.ap()).then_inc(swl, 16)
                s.dma_start(out=mask_sb[:, :, :], in_=mask_h.ap()).then_inc(swl, 16)
                s.dma_start(out=gvt_sb[:, :], in_=gvt_h.ap()).then_inc(swl, 16)
                for tt in range(NTILE):
                    if tt >= 2:
                        s.wait_ge(sXQ, tt - 1)
                    s.dma_start(out=sx[tt % 2][:, :], in_=x_t[8 * tt:8 * (tt + 1)]).then_inc(sxlp[tt % 2], 16)

            @blk.gpsimd
            def _(g):
                g.dma_start(out=wq_sb[:, :, :], in_=wqt_h.ap().rearrange("(k p) m -> p k m", p=128)).then_inc(swlg, 16)
                g.dma_start(out=wk_sb[:, :, :], in_=wkt_h.ap().rearrange("(k p) m -> p k m", p=128)).then_inc(swlg, 16)
                g.dma_start(out=wv_sb[:, :, :], in_=wvt_h.ap().rearrange("(k p) m -> p k m", p=128)).then_inc(swlg, 16)
                g.dma_start(out=wot_bf[:, :, :], in_=wot_h.ap().rearrange("(t p) h -> p t h", p=128)).then_inc(swlg, 16)
                for T in range(32):
                    if T >= 2:
                        g.wait_ge(sWQD, T - 1)
                    g.dma_start(out=swf[T % 2][:, :], in_=wfc_h.ap()[128 * T:128 * (T + 1), :]).then_inc(swfl2[T % 2], 16)
                g.wait_ge(sZC, 1)
                g.dma_start(out=zin_h.ap().rearrange("(t p) m -> p t m", p=128), in_=zbf[:, :, :]).then_inc(sCCD, 16)
                g.wait_ge(sCCD, 16)
                g.collective_compute(
                    "AllGather", ALU.bypass,
                    replica_groups=[list(range(NCORE))],
                    ins=[zin_h.ap()],
                    outs=[zout_h.ap()],
                ).then_inc(sCC)
                g.wait_ge(sCC, 1)
                for j in range(NCORE):
                    if j >= 2:
                        g.wait_ge(sSTI, j - 1)
                    g.dma_start(out=Zgt[j % 2][:, :, :],
                                in_=zout_h.ap()[M * j:M * (j + 1), :].rearrange("(t p) m -> p t m", p=128)).then_inc(sCCz[j % 2], 16)

            @blk.vector
            def _(v):
                view3 = lambda a: a.rearrange("p (b e) -> p b e", b=32)
                bc = lambda a: a.unsqueeze(2).broadcast_to([128, 32, 32])
                for c in range(NCHUNK):
                    for j in range(TPC):
                        tt = TPC * c + j
                        v.wait_ge(sxlp[tt % 2], 16 * (tt // 2 + 1))
                        if tt >= 1:
                            v.wait_ge(sRSD, tt)
                        v.bn_stats(bst[:, 0, :], sx[tt % 2][:, 0:512])
                        v.bn_stats(bst[:, 1, :], sx[tt % 2][:, 512:1024])
                        v.drain()
                        v.bn_aggr(bag[:, :], bst[:, :, :])
                        v.drain()
                        v.engine_nop().then_inc(sBNA, 1)
                        v.wait_ge(sRSD, tt + 1)
                        v.scalar_tensor_tensor(negmr[:, :], bag[:, 0:1], -1.0, srstd[:, :], ALU.mult, ALU.mult)
                        v.drain()
                        v.engine_nop().then_inc(sNM, 1)
                        v.wait_ge(sAPL, tt + 1)
                        quant_f32(v, 2, sx[tt % 2][:, :], sxq[tt % 2][:, :], habs[:, :],
                                  view3, bc, view3, bc,
                                  l1[:, :, :], am_bf[:, :], am32[:, :], acl[:, :],
                                  apw[:, :], asc[:, :], ain[:, :], asc16[:, :],
                                  amax_src16=sxn[tt % 2][:, :],
                                  fuse=(srstd[:, :], negmr[:, :], rinv[:, :], nminv[:, :]))
                        v.engine_nop().then_inc(sXQ, 1)
                        v.wait_ge(swfl2[tt % 2], 16 * (tt // 2 + 1))
                        if tt >= 1:
                            v.wait_ge(sWT, 16 * tt)
                        quant_f32(v, 2, swf[tt % 2][:, :], wq_bf[:, :], habs[:, :], view3, bc, view3, bc,
                                  l1[:, :, :], am_bf[:, :], am32[:, :], acl[:, :],
                                  apw[:, :], asc[:, :], ain[:, :], asc16[:, :])
                        v.engine_nop().then_inc(sWQD, 1)
                        if tt >= 1:
                            v.tensor_copy(wfcT[:, :, 128 * (tt - 1):128 * tt], tst[(tt - 1) % 2][:, :, :])
                            v.engine_nop().then_inc(sW8, 1)
                        v.wait_ge(sXQT, 16 * (tt + 1))
                        v.tensor_copy(xqT[:, :, 128 * tt:128 * (tt + 1)], xst[tt % 2][:, :, :])
                        v.engine_nop().then_inc(sXQ8, 1)
                    v.wait_ge(sQMM, c + 1)
                    for mt in range(2):
                        v.tensor_tensor(qT_all[:, mt, TOKC * c:TOKC * (c + 1)],
                                        P1[:, 512 * mt:512 * (mt + 1)], dq_sb[:, :], ALU.mult)
                    v.engine_nop().then_inc(sQE, 1)
                    v.wait_ge(sKMM, c + 1)
                    for mt in range(2):
                        v.tensor_tensor(ktT[:, mt, :],
                                        P1[:, 1024 + 512 * mt:1024 + 512 * (mt + 1)], dk_sb[:, :], ALU.mult)
                    v.engine_nop().then_inc(sKE, 1)
                    v.wait_ge(sAMM, c + 1)
                    for ut in range(TPC):
                        v.tensor_tensor(A_sb[:, ut, :], P1[:, 512 * ut:512 * (ut + 1)],
                                        mask_sb[:, ut, :], ALU.mult)
                    v.engine_nop().then_inc(sAE, 1)

                v.wait_ge(sWT, 16 * 32)
                v.tensor_copy(wfcT[:, :, 128 * 31:128 * 32], tst[31 % 2][:, :, :])
                v.engine_nop().then_inc(sW8, 1)
                v.memset(Za[:, :, :], 0.0)
                v.drain()
                for c in range(NCHUNK):
                    v.wait_ge(sUE, c + 1)
                    v.tensor_tensor(Zb[:, :, :], Za[:, :, :], U_all[:, :, c, :], ALU.add)
                    v.drain()
                    v.tensor_single_scalar(Za[:, :, :], Zb[:, :, :], DEC32, ALU.mult)
                    v.drain()
                v.tensor_copy(zbf[:, :, :], Za[:, :, :])
                v.drain()
                v.engine_nop().then_inc(sZC, 1)
                v.wait_ge(swl, 112)
                v.tensor_scalar(STa[:, :, :], m0_sb[:, :, :], gvt_sb[:, 8:9], None, ALU.mult)
                v.drain()
                for j in range(NCORE):
                    v.wait_ge(sCCz[j % 2], 16 * (j // 2 + 1))
                    v.tensor_scalar(Zb[:, :, :], Zgt[j % 2][:, :, :], gvt_sb[:, j:j + 1], None, ALU.mult)
                    v.drain()
                    v.engine_nop().then_inc(sSTI, 1)
                    v.tensor_tensor(STa[:, :, :], STa[:, :, :], Zb[:, :, :], ALU.add)
                    v.drain()
                for c in range(NCHUNK):
                    v.tensor_copy(st_bf[:, :, :], STa[:, :, :])
                    v.drain()
                    v.engine_nop().then_inc(sSTB, 1)
                    v.tensor_tensor(STb[:, :, :], STa[:, :, :], U_all[:, :, c, :], ALU.add)
                    v.drain()
                    v.tensor_single_scalar(STa[:, :, :], STb[:, :, :], DEC32, ALU.mult)
                    v.drain()
                    v.wait_ge(sRIM, c + 1)
                    for mt in range(2):
                        v.tensor_tensor(rT[:, mt, TOKC * c:TOKC * (c + 1)],
                                        rT[:, mt, TOKC * c:TOKC * (c + 1)],
                                        P2[:, 512 * mt:512 * (mt + 1)], ALU.add)
                    v.engine_nop().then_inc(sRIE, 1)

            @blk.scalar
            def _(s):
                for c in range(NCHUNK):
                    for j in range(TPC):
                        tt = TPC * c + j
                        s.wait_ge(sBNA, tt + 1)
                        s.activation(slnv[:, :], bag[:, 1:2], AF.Ln, bias=eps_col[:, :])
                        s.drain()
                        s.activation(srstd[:, :], slnv[:, :], AF.Exp, scale=-0.5)
                        s.drain()
                        s.nop().then_inc(sRSD, 1)
                        s.wait_ge(sNM, tt + 1)
                        if tt >= 2:
                            s.wait_ge(sXNM, tt - 1)
                            s.wait_ge(sXQ, tt - 1)
                        s.activation(sxn[tt % 2][:, :], sx[tt % 2][:, :], AF.Identity,
                                     bias=negmr[:, :], scale=srstd[:, :]).then_inc(sAPL, 1)
                        s.wait_ge(sXNM, tt + 1)
                        if j == 0 and c >= 1:
                            s.wait_ge(sVMM, c)
                        s.activation(xnT[:, :, 128 * j:128 * (j + 1)],
                                     PT[:, :].rearrange("p (k c2) -> p k c2", k=8),
                                     AF.Identity).then_inc(sXNE, 1)
                        s.wait_ge(sXQ, tt + 1)
                        if tt >= 2:
                            s.wait_ge(sXQ8, tt - 1)
                        s.dma_start(out=xst[tt % 2][:, :, :],
                                    in_=sxq[tt % 2][:, :].rearrange("p (k c2) -> p k c2", k=8),
                                    transpose=True).then_inc(sXQT, 16)
                        s.wait_ge(sWQD, tt + 1)
                        if tt >= 2:
                            s.wait_ge(sW8, tt - 1)
                        s.dma_start(out=tst[tt % 2][:, :, :],
                                    in_=wq_bf[:, :].rearrange("p (k c) -> p k c", k=8),
                                    transpose=True).then_inc(sWT, 16)
                    s.wait_ge(sVMM, c + 1)
                    s.activation(v_sb[:, :, :], P2[:, :].rearrange("p (a b) -> p a b", a=TPC), AF.Identity).then_inc(sVE, 1)
                    s.wait_ge(sKE, c + 1)
                    if c >= 1:
                        s.wait_ge(sUMM, c)
                    for nt in range(2):
                        s.dma_start(out=ktc[:, :, 128 * nt:128 * (nt + 1)],
                                    in_=ktT[:, nt, :].rearrange("p (u c2) -> p u c2", u=TPC),
                                    transpose=True).then_inc(sKTC, 16)
                    s.wait_ge(sRMM, c + 1)
                    s.activation(rT[:, :, TOKC * c:TOKC * (c + 1)],
                                 P2[:, :].rearrange("p (a b) -> p a b", a=2), AF.Identity).then_inc(sRE, 1)
                    s.wait_ge(sUMM, c + 1)
                    s.activation(U_all[:, :, c, :], P3[:, :].rearrange("p (a b) -> p a b", a=2), AF.Identity).then_inc(sUE, 1)

            @blk.tensor
            def _(t):
                t.wait_ge(sWCV, 1)
                t.wait_ge(swl, 112)
                t.wait_ge(swlg, 112)
                for c in range(NCHUNK):
                    for j in range(TPC):
                        tt = TPC * c + j
                        t.wait_ge(sAPL, tt + 1)
                        if tt >= 1:
                            t.wait_ge(sXNE, tt)
                        for k in range(8):
                            i = t.transpose(PT[:, 128 * k:128 * (k + 1)],
                                            sxn[tt % 2][:, 128 * k:128 * (k + 1)], idn_sb[:, :])
                        i.then_inc(sXNM, 1)
                    t.wait_ge(sXNE, TPC * (c + 1))
                    if c >= 1:
                        t.wait_ge(sAE, c)
                        t.wait_ge(sKE, c)
                    for mt in range(2):
                        for k in range(8):
                            i = t.matmul(P1[:, 512 * mt:512 * (mt + 1)], wq_sb[:, k, 128 * mt:128 * (mt + 1)],
                                         xnT[:, k, :], start=(k == 0), stop=(k == 7))
                    i.then_inc(sQMM, 1)
                    for mt in range(2):
                        for k in range(8):
                            i = t.matmul(P1[:, 1024 + 512 * mt:1024 + 512 * (mt + 1)], wk_sb[:, k, 128 * mt:128 * (mt + 1)],
                                         xnT[:, k, :], start=(k == 0), stop=(k == 7))
                    i.then_inc(sKMM, 1)
                    if c >= 1:
                        t.wait_ge(sRE, c)
                    for vt in range(TPC):
                        for k in range(8):
                            i = t.matmul(P2[:, 256 * vt:256 * (vt + 1)], xnT[:, k, 128 * vt:128 * (vt + 1)],
                                         wv_sb[:, k, :], start=(k == 0), stop=(k == 7))
                    i.then_inc(sVMM, 1)
                    t.wait_ge(sQE, c + 1)
                    t.wait_ge(sKE, c + 1)
                    for ut in range(TPC):
                        for km in range(2):
                            i = t.matmul(P1[:, 512 * ut:512 * (ut + 1)], ktT[:, km, 128 * ut:128 * (ut + 1)],
                                         qT_all[:, km, TOKC * c:TOKC * (c + 1)], start=(km == 0), stop=(km == 1))
                    i.then_inc(sAMM, 1)
                    t.wait_ge(sAE, c + 1)
                    t.wait_ge(sVE, c + 1)
                    for mt in range(2):
                        for ut in range(TPC):
                            i = t.matmul(P2[:, 512 * mt:512 * (mt + 1)], v_sb[:, ut, 128 * mt:128 * (mt + 1)],
                                         A_sb[:, ut, :], start=(ut == 0), stop=(ut == 3))
                    i.then_inc(sRMM, 1)
                    t.wait_ge(sKTC, 32 * (c + 1))
                    if c >= 1:
                        t.wait_ge(sUE, c)
                    for nt in range(2):
                        for ut in range(TPC):
                            i = t.matmul(P3[:, 256 * nt:256 * (nt + 1)], ktc[:, ut, 128 * nt:128 * (nt + 1)],
                                         v_sb[:, ut, :], start=(ut == 0), stop=(ut == 3))
                    i.then_inc(sUMM, 1)
                for c in range(NCHUNK):
                    t.wait_ge(sSTB, c + 1)
                    if c == 0:
                        t.wait_ge(sRE, NCHUNK)
                    else:
                        t.wait_ge(sRIE, c)
                    for mt in range(2):
                        for nt in range(2):
                            i = t.matmul(P2[:, 512 * mt:512 * (mt + 1)], st_bf[:, nt, 128 * mt:128 * (mt + 1)],
                                         qT_all[:, nt, TOKC * c:TOKC * (c + 1)], start=(nt == 0), stop=(nt == 1))
                    i.then_inc(sRIM, 1)

        wprojT = sb("wprojT", [128, 32, D], F8)
        with ExitStack() as c0:
            sbc = lambda n, s, d: c0.enter_context(nc.sbuf_tensor(n, s, d))
            swp = [sbc("swp0", [128, H], F32)]
            wq_bf = sbc("c_wq", [128, H], F16)
            habs = sbc("c_habs", [128, H], F16)
            l1 = sbc("c_l1", [128, 16, 128], F16)
            am_bf = sbc("c_am", [128, 128], F16)
            am32 = sbc("c_am32", [128, 128], F32)
            acl = sbc("c_acl", [128, 128], F32)
            apw = sbc("c_apw", [128, 128], F32)
            asc = sbc("c_asc", [128, 128], F32)
            ain = sbc("c_ain", [128, 128], F32)
            asc16 = sbc("c_asc16", [128, 128], F16)
            ain16 = sbc("c_ain16", [128, 128], F16)
            wpst = [sbc(f"c_wpst{i}", [128, 32, 128], F16) for i in range(2)]
            blk = c0.enter_context(nc.Block("c0"))

            @blk.gpsimd
            def _(g):
                for T in range(8):
                    if T >= 1:
                        g.wait_ge(sPQ, T)
                    g.dma_start(out=swp[0][:, :], in_=wproj_h.ap()[128 * T:128 * (T + 1), :]).then_inc(sWPL2[T % 2], 16)

            @blk.vector
            def _(v):
                view3 = lambda a: a.rearrange("p (b e) -> p b e", b=32)
                bc = lambda a: a.unsqueeze(1).broadcast_to([128, 32, 128])
                bc32 = bc
                for T in range(8):
                    v.wait_ge(sWPL2[T % 2], 16 * (T // 2 + 1))
                    if T >= 1:
                        v.wait_ge(sWPT, 16 * T)
                    quant_f32(v, 1, swp[0][:, :], wq_bf[:, :], habs[:, :], view3, bc, view3, bc32,
                              l1[:, :, :], am_bf[:, :], am32[:, :], acl[:, :],
                              apw[:, :], asc[:, :], ain[:, :], asc16[:, :])
                    v.engine_nop().then_inc(sPQ, 1)

            @blk.scalar
            def _(s):
                for T in range(8):
                    s.wait_ge(sPQ, T + 1)
                    if T >= 2:
                        s.wait_ge(sWP8, T - 1)
                    s.dma_start(out=wpst[T % 2][:, :, :],
                                in_=wq_bf[:, :].rearrange("p (k c2) -> p k c2", k=32),
                                transpose=True).then_inc(sWPT, 16)
                    s.wait_ge(sWPT, 16 * (T + 1))
                    s.activation(wprojT[:, :, 128 * T:128 * (T + 1)], wpst[T % 2][:, :, :],
                                 AF.Identity).then_inc(sWP8, 1)

        with ExitStack() as c1:
            sbc = lambda n, s, d: c1.enter_context(nc.sbuf_tensor(n, s, d))
            h_bf = sbc("h_bf", [128, H], F16)
            hb2 = sbc("hb2", [128, H], F16)
            l1 = sbc("f_l1", [128, 16, 128], F16)
            am_bf = sbc("f_am", [128, 128], F16)
            am32 = sbc("f_am32", [128, 128], F32)
            acl = sbc("f_acl", [128, 128], F32)
            apw = sbc("f_apw", [128, 128], F32)
            asc = sbc("f_asc", [128, 128], F32)
            ain = sbc("f_ain", [128, 128], F32)
            asc16 = sbc("f_asc16", [128, 128], F16)
            ain16 = sbc("f_ain16", [128, 128], F16)
            hstage = sbc("hstage", [128, 32, 128], F16)
            hstage8 = sbc("hstage8", [128, 32, 128], F8)
            tnh = sbc("tnh", [128, D], F16)
            g_bf = sbc("g_bf", [128, D], F16)
            sx2 = sbc("sx2", [128, D], F32)
            sout = sbc("sout", [128, D], F32)
            PH = c1.enter_context(nc.psum_tensor("PH", [128, 2048], F32))
            PH2 = c1.enter_context(nc.psum_tensor("PH2", [128, 2048], F32))
            blk = c1.enter_context(nc.Block("c1"))
            PO = PH2[:, 0:1024]
            PS2 = PH2[:, 1024:2048]

            x_t = x_h.ap().transpose([1, 0, 2])
            o_t = out_h.ap().transpose([1, 0, 2])

            def mm1_half(t, tile, half):
                for k in range(4):
                    for jj in range(4):
                        t.matmul(PH[:, 512 * jj:512 * (jj + 1)],
                                 xqT[:, 2 * k:2 * k + 2, 128 * tile:128 * (tile + 1)],
                                 wfcT[:, 2 * k:2 * k + 2, 2048 * half + 512 * jj:2048 * half + 512 * (jj + 1)],
                                 start=(k == 0), stop=False, perf_mode=PM)
                for jj in range(4):
                    i = t.matmul(PH[:, 512 * jj:512 * (jj + 1)], ones_r,
                                 bfc_r[:, 2048 * half + 512 * jj:2048 * half + 512 * (jj + 1)],
                                 start=False, stop=True)
                i.then_inc(sH1, 1)

            def mm2_smm(t, u):
                if u >= 1:
                    t.wait_ge(sCMB, u)
                for nt in range(2):
                    for mt in range(2):
                        i = t.matmul(PS2[:, 512 * nt:512 * (nt + 1)],
                                     rT[:, mt, 128 * u:128 * (u + 1)],
                                     wot_bf[:, mt, 512 * nt:512 * (nt + 1)],
                                     start=(mt == 0), stop=(mt == 1))
                i.then_inc(sSMM, 1)
                t.wait_ge(sH8, u + 1)
                if u >= 1:
                    t.wait_ge(sSIL, u)
                for k in range(16):
                    for nt in range(2):
                        t.matmul(PO[:, 512 * nt:512 * (nt + 1)], hstage8[:, 2 * k:2 * k + 2, :],
                                 wprojT[:, 2 * k:2 * k + 2, 512 * nt:512 * (nt + 1)],
                                 start=(k == 0), stop=False, perf_mode=PM)
                for nt in range(2):
                    i = t.matmul(PO[:, 512 * nt:512 * (nt + 1)], ones_r,
                                 bpr_r[:, 512 * nt:512 * (nt + 1)], start=False, stop=True)
                i.then_inc(sPOM, 1)

            @blk.sync
            def _(s):
                for tt in range(NTILE):
                    if tt >= 1:
                        s.wait_ge(sCMB, tt)
                    s.dma_start(out=sx2[:, :], in_=x_t[8 * tt:8 * (tt + 1)]).then_inc(sxl2, 16)
                    s.wait_ge(sHQ, tt + 1)
                    if tt >= 1:
                        s.wait_ge(sH8, tt)
                    s.dma_start(out=hstage[:, :, :],
                                in_=hb2[:, :].rearrange("p (k c2) -> p k c2", k=32),
                                transpose=True).then_inc(sHQT, 16)
                    if tt >= 1:
                        s.wait_ge(sCMB, tt)
                        s.dma_start(out=o_t[8 * (tt - 1):8 * tt], in_=sout[:, :]).then_inc(sOUT, 16)
                s.wait_ge(sCMB, NTILE)
                s.dma_start(out=o_t[8 * (NTILE - 1):8 * NTILE], in_=sout[:, :]).then_inc(sOUT, 16)
                s.wait_ge(sOUT, 16 * NTILE)

            @blk.tensor
            def _(t):
                t.wait_ge(sW8, 32)
                t.wait_ge(sXQ8, NTILE)
                t.wait_ge(sWP8, 8)
                t.wait_ge(swlg, 112)
                for tt in range(NTILE + 1):
                    if tt < NTILE:
                        t.wait_ge(sGEL, 2 * tt)
                        mm1_half(t, tt, 0)
                    if tt >= 1:
                        mm2_smm(t, tt - 1)
                    if tt < NTILE:
                        t.wait_ge(sGEL, 2 * tt + 1)
                        mm1_half(t, tt, 1)

            @blk.scalar
            def _(s):
                for tt in range(NTILE):
                    s.wait_ge(sH1, 2 * tt + 1)
                    if tt >= 1:
                        s.wait_ge(sHQ, tt)
                    s.activation(h_bf[:, 0:2048], PH[:, :], AF.Gelu).then_inc(sGEL, 1)
                    if tt >= 1:
                        s.wait_ge(sHQT, 16 * tt)
                        if tt >= 2:
                            s.wait_ge(sPOM, tt - 1)
                        s.activation(hstage8[:, :, :], hstage[:, :, :], AF.Identity).then_inc(sH8, 1)
                        s.wait_ge(sPOM, tt)
                        s.activation(tnh[:, :], PO, AF.Tanh, scale=0.5).then_inc(sTAN, 1)
                    s.wait_ge(sH1, 2 * tt + 2)
                    s.activation(h_bf[:, 2048:4096], PH[:, :], AF.Gelu).then_inc(sGEL, 1)
                s.wait_ge(sHQT, 16 * NTILE)
                s.wait_ge(sPOM, NTILE - 1)
                s.activation(hstage8[:, :, :], hstage[:, :, :], AF.Identity).then_inc(sH8, 1)
                s.wait_ge(sPOM, NTILE)
                s.activation(tnh[:, :], PO, AF.Tanh, scale=0.5).then_inc(sTAN, 1)

            @blk.vector
            def _(v):
                view3 = lambda a: a.rearrange("p (b e) -> p b e", b=32)
                bc = lambda a: a.unsqueeze(1).broadcast_to([128, 32, 128])
                for tt in range(NTILE + 1):
                    if tt >= 1:
                        u = tt - 1
                        v.wait_ge(sTAN, u + 1)
                        v.tensor_scalar(tnh[:, :], tnh[:, :], 0.5, 0.5, ALU.mult, ALU.add)
                        v.drain()
                        v.tensor_tensor(g_bf[:, :], tnh[:, :], PO, ALU.mult)
                        v.drain()
                        v.engine_nop().then_inc(sSIL, 1)
                        v.wait_ge(sSMM, u + 1)
                        v.wait_ge(sxl2, 16 * (u + 1))
                        if u >= 1:
                            v.wait_ge(sOUT, 16 * u)
                        v.scalar_tensor_tensor(sout[:, :], PS2, sgate_col[:, :], g_bf[:, :], ALU.mult, ALU.mult)
                        v.drain()
                        v.tensor_tensor(sout[:, :], sout[:, :], sx2[:, :], ALU.add)
                        v.engine_nop().then_inc(sCMB, 1)
                    if tt < NTILE:
                        v.wait_ge(sGEL, 2 * tt + 2)
                        if tt >= 1:
                            v.wait_ge(sHQT, 16 * tt)
                        quant_chain(v, 1, h_bf[:, :], hb2[:, :], view3, bc,
                                    l1[:, :, :], am_bf[:, :], am32[:, :], acl[:, :],
                                    apw[:, :], asc[:, :], ain[:, :], asc16[:, :], ain16[:, :], habs=None)
                        v.engine_nop().then_inc(sHQ, 1)

    return nc


_PERM = None
_NC = None


def kernel(L0_attn_x, L0_attn_Wq, L0_attn_Wk, L0_attn_Wv, L0_attn_Wo, L0_attn_gate,
           L0_attn_M0, L0_ffn_w_fc, L0_ffn_b_fc, L0_ffn_w_proj, L0_ffn_b_proj):
    global _PERM, _NC
    if _PERM is None:
        p = np.arange(H)
        _PERM = 32 * (p % 128) + p // 128
    perm = _PERM
    x = np.ascontiguousarray(L0_attn_x)

    wqt = np.ascontiguousarray(np.asarray(L0_attn_Wq).T)
    wkt = np.ascontiguousarray(np.asarray(L0_attn_Wk).T)
    wvt = np.ascontiguousarray(np.asarray(L0_attn_Wv).T)
    wot = np.ascontiguousarray(np.asarray(L0_attn_Wo).T)
    m0t = np.ascontiguousarray(np.asarray(L0_attn_M0).T)
    gate = np.ascontiguousarray(np.asarray(L0_attn_gate).reshape(1, 1))
    wfc = np.ascontiguousarray(np.asarray(L0_ffn_w_fc)[perm, :])
    bfc = np.ascontiguousarray(np.asarray(L0_ffn_b_fc)[perm].reshape(1, H))
    wproj = np.ascontiguousarray(np.asarray(L0_ffn_w_proj)[:, perm])
    bproj = np.ascontiguousarray(np.asarray(L0_ffn_b_proj).reshape(1, D))

    tl = (np.arange(TOKC) // B).astype(np.float64)
    dq = np.ascontiguousarray(np.broadcast_to(
        (np.float32(DEC) ** tl).astype(np.float16), (128, TOKC)))
    dk = np.ascontiguousarray(np.broadcast_to(
        ((np.float32(DEC) ** (-1.0 - tl)) / np.float32(B)).astype(np.float16), (128, TOKC)))
    maskt = np.zeros((128, TPC, TOKC), dtype=np.float16)
    for ut in range(TPC):
        uu = ut * 128 + np.arange(128)
        maskt[:, ut, :] = (tl[uu][:, None] < tl[None, :]).astype(np.float16)
    ones = np.ones((1, 128), np.float32)
    idn = np.eye(128, dtype=np.float16)

    if _NC is None:
        _NC = build_nc()

    in_maps = []
    for i in range(NCORE):
        gv = np.zeros((9,), np.float32)
        for j in range(i):
            gv[j] = np.float32(DEC) ** np.float32(256 * (i - 1 - j))
        gv[8] = np.float32(DEC) ** np.float32(256 * i)
        gvt = np.ascontiguousarray(np.broadcast_to(gv, (128, 9)))
        in_maps.append({
            "x": np.ascontiguousarray(x[:, TLOC * i:TLOC * (i + 1), :]),
            "wqt": wqt, "wkt": wkt, "wvt": wvt, "wot": wot, "m0t": m0t, "gate": gate,
            "wfc": wfc, "bfc": bfc, "wproj": wproj, "bproj": bproj,
            "dq": dq, "dk": dk, "maskt": maskt, "gvt": gvt, "onesr": ones, "idn": idn,
        })
    res = run_bass_kernel_spmd(_NC, in_maps, core_ids=list(range(NCORE)))
    return np.concatenate([np.asarray(res.results[i]["out"]) for i in range(NCORE)], axis=1)
